# version 18 (final)
# baseline (speedup 1.0000x reference)
import contextlib
import os
import numpy as np
import ml_dtypes
import concourse.bass as bass
import concourse.mybir as mybir
from concourse.bass_utils import run_bass_kernel_spmd

F32 = mybir.dt.float32
BF16 = mybir.dt.bfloat16
U8 = mybir.dt.uint8
ALU = mybir.AluOpType
AF = mybir.ActivationFunctionType
AX = mybir.AxisListType

D = 1024
C2 = 2048
NST = 16
TV = 4096
TH = 2048
TA = 256
NCH = TV // TA
NRE = TH // TA
TC = 512
NCC = TH // TC
CT = C2 // 128
DT = D // 128
NEXP = 16
EPS = 1e-5


class _Chained:
    def then_inc(self, *a):
        return self


_CHAINED = _Chained()


class Ctr:
    def __init__(self, es, nc, name):
        self.sem = es.enter_context(nc.semaphore(name))
        self.n = 0

    def inc(self, inst, k=1):
        if inst is _CHAINED:
            return inst
        if k:
            inst.then_inc(self.sem, k)
            self.n += k
        return inst


_CHAIN_OPS = {
    "tensor_copy", "tensor_tensor", "tensor_add", "tensor_sub", "tensor_mul",
    "tensor_max", "tensor_scalar", "tensor_single_scalar", "tensor_scalar_add",
    "tensor_scalar_sub", "tensor_scalar_mul", "tensor_scalar_max",
    "tensor_scalar_min", "tensor_relu", "tensor_reduce", "reduce_max",
    "reduce_sum", "scalar_tensor_tensor", "tensor_tensor_scan", "memset",
    "select", "copy_predicated", "reciprocal", "activation", "copy", "mul", "add", "square",
    "sqrt", "sign",
}


class DmaQ:

    def __init__(self, eng, ctr):
        self._e = eng
        self._c = ctr

    def wait_ge(self, sem, v):
        return self._e.wait_ge(sem, v)

    def dma_start(self, *a, **k):
        self._e.wait_ge(self._c.sem, self._c.n)
        inst = self._e.dma_start(*a, **k)
        inst.then_inc(self._c.sem, 16)
        self._c.n += 16
        return _CHAINED

    def __getattr__(self, name):
        return getattr(self._e, name)


class Rot:

    def __init__(self, es, nc, name, depth=4):
        self.sems = [es.enter_context(nc.semaphore(f"{name}{i}"))
                     for i in range(depth)]
        self.ns = [0] * depth
        self.i = -1

    def begin(self):
        self.i = (self.i + 1) % len(self.sems)

    def dma(self, eng, **k):
        eng.dma_start(**k).then_inc(self.sems[self.i], 16)
        self.ns[self.i] += 16

    def tok(self):
        return (self.sems[self.i], self.ns[self.i])


class Chain:

    def __init__(self, eng, ctr):
        self._e = eng
        self._c = ctr

    def wait_ge(self, sem, v):
        return self._e.wait_ge(sem, v)

    def dma_start(self, *a, **k):
        return self._e.dma_start(*a, **k)

    def __getattr__(self, name):
        fn = getattr(self._e, name)
        if name in _CHAIN_OPS:
            def wrap(*a, **k):
                self._e.wait_ge(self._c.sem, self._c.n)
                inst = fn(*a, **k)
                inst.then_inc(self._c.sem, 1)
                self._c.n += 1
                return _CHAINED
            return wrap
        return fn


def build_nc(debug=False, phases=5):
    nc = bass.Bass("TRN2", target_bir_lowering=False)
    es = contextlib.ExitStack()
    nc._es = es

    dm = lambda n, s, t, k="ExternalInput": nc.dram_tensor(n, s, t, kind=k).ap()
    xT = dm("xT", [D, TV], F32)
    flag = dm("flag", [1, 1], F32)
    W_in = dm("W_in", [D, 2 * C2], F32)
    convd = dm("convd", [96, 128, 128], F32)
    conv_b = dm("conv_b", [C2], F32)
    W_xproj = dm("W_xproj", [C2, 2 * NST], F32)
    W_dt = dm("W_dt", [C2, C2], F32)
    b_dt = dm("b_dt", [C2], F32)
    A_log = dm("A_log", [NST], F32)
    D_param = dm("D_param", [C2], F32)
    W_out = dm("W_out", [C2, D], F32)
    centT = dm("centT", [D, NST], F32)
    eb = dm("eb", [NST], F32)
    WrT = dm("WrT", [NEXP, D, D], F32)
    br = dm("br", [NEXP, D], F32)
    WsT = dm("WsT", [2, D, D], F32)
    bs = dm("bs", [2, D], F32)
    id16f = dm("id16f", [16, 16], F32)
    id128f = dm("id128f", [128, 128], F32)

    spk = "ExternalOutput" if debug else "Internal"
    u_sp = dm("u_sp", [NCH, 128, CT * TA], F32, spk)
    xib_sp = dm("xib_sp", [NCH, 128, CT * TA], BF16, spk)
    sres_sp = dm("sres_sp", [NCH, 128, CT * TA], F32, spk)
    d_sp = dm("d_sp", [NCH, 128, CT * TA], F32, spk)
    g_sp = dm("g_sp", [NCH, 128, CT * TA], BF16, spk)
    u2_sp = dm("u2_sp", [NCC, 128, DT * TC], F32, spk)
    bc_hbm = dm("bc_hbm", [32, TV], F32, spk)
    ffnin_sp = dm("ffnin_sp", [NCC, 128, DT * TC], F32, spk)
    gates_hbm = dm("gates_hbm", [16, TH], F32, spk)
    oT = dm("oT", [D, TH], F32, "ExternalOutput")

    sb = lambda n, s, t: es.enter_context(nc.sbuf_tensor(n, s, t))
    BC_buf = sb("BC_buf", [32, TV], F32)
    A_bc = sb("A_bc", [128, NST], F32)
    A_row = sb("A_row", [1, NST], F32)
    flag_col = sb("flag_col", [128, 1], F32)
    bdt_c = sb("bdt_c", [128, CT], F32)
    cvb_c = sb("cvb_c", [128, CT], F32)
    Dp_c = sb("Dp_c", [128, CT], F32)
    bsum_c = sb("bsum_c", [128, DT], F32)
    bs_t = sb("bs_t", [128, DT], F32)
    eb_col = sb("eb_col", [16, 1], F32)
    sid16 = sb("sid16", [16, 16], F32)
    sid128 = sb("sid128", [128, 128], F32)
    ones1 = sb("ones1", [128, 1], F32)
    onesr = sb("onesr", [1, 128], F32)
    onesb = sb("onesb", [128, TA], BF16)
    neginf = sb("neginf", [128, NST], F32)
    carry = sb("carry", [128, NST * CT], F32)
    scent = sb("scent", [128, DT * NST], BF16)
    u_b = sb("u_b", [128, CT * TA], F32)
    sres_b = sb("sres_b", [128, CT * TA], F32)
    d_b = sb("d_b", [128, CT * TA], F32)
    du_b = sb("du_b", [128, CT * TA], F32)

    PP = es.enter_context(nc.psum_tensor("PP", [128, 4096], F32))
    pb = [PP[:, i * 512:(i + 1) * 512] for i in range(8)]

    cL = Ctr(es, nc, "cL")
    cS = Ctr(es, nc, "cS")
    cT = Ctr(es, nc, "cT")
    cA = Ctr(es, nc, "cA")
    cV = Ctr(es, nc, "cV")
    cPh = Ctr(es, nc, "cPh")
    rA = Rot(es, nc, "rA")
    rB = Rot(es, nc, "rB")
    rSl = Rot(es, nc, "rSl")
    rY = Rot(es, nc, "rY")
    rC = Rot(es, nc, "rC")
    rDu = Rot(es, nc, "rDu")
    ROTS = [rA, rB, rSl, rY, rC, rDu]

    blk = es.enter_context(nc.Block())

    def emit(_):
        te = nc.tensor
        ve = Chain(nc.vector, cV)
        se = Chain(nc.scalar, cA)
        ge = DmaQ(nc.gpsimd, cL)
        sy = DmaQ(nc.sync, cS)
        geE = nc.gpsimd
        syE = nc.sync

        pool_state = [None] * 8
        pool_rr = [0]

        def acquire_bank():
            i = pool_rr[0] % 8
            pool_rr[0] += 1
            w = pool_state[i]
            if w is not None:
                te.wait_ge(w[0].sem, w[1])
            pool_state[i] = None
            return i

        def release_bank(i, ctr):
            pool_state[i] = (ctr, ctr.n)

        def acquire2():
            if pool_rr[0] % 2 == 1:
                pool_rr[0] += 1
            i = pool_rr[0] % 8
            pool_rr[0] += 2
            for j in (i, i + 1):
                w = pool_state[j]
                if w is not None:
                    te.wait_ge(w[0].sem, w[1])
                pool_state[j] = None
            return i

        def release2(i, ctr):
            pool_state[i] = (ctr, ctr.n)
            pool_state[i + 1] = (ctr, ctr.n)

        def pool_reset():
            for i in range(8):
                w = pool_state[i]
                if w is not None:
                    te.wait_ge(w[0].sem, w[1])
                pool_state[i] = None
            pool_rr[0] = 0

        def pairview(i, w):
            return PP[:, i * 512:(i + 2) * 512].rearrange(
                "p (b n) -> p b n", n=512)[:, :, 0:w]

        def ph_barrier():
            for eng in (te, ve, se, ge, sy):
                for ctr in (cT, cV, cA, cL, cS):
                    eng.wait_ge(ctr.sem, ctr.n)
                for rot in ROTS:
                    for sem, n in zip(rot.sems, rot.ns):
                        if n:
                            eng.wait_ge(sem, n)

        def emit_ln(in_b, out_b, W, sq_b, stat, mu_bc, rs_bc):
            i3 = in_b.rearrange("p (d t) -> p d t", t=W)
            se.wait_ge(cV.sem, cV.n)
            se.wait_ge(cS.sem, cS.n)
            se.wait_ge(cT.sem, cT.n)
            cA.inc(se.activation(out=sq_b[:, 0:DT * W], in_=in_b, func=AF.Square))
            sq_A = cA.n
            b1 = acquire_bank()
            b2 = acquire_bank()
            te.wait_ge(cA.sem, sq_A)
            for dt_i in range(DT):
                cT.inc(te.matmul(pb[b1][0:1, 0:W], ones1[:], i3[:, dt_i, :],
                                 start=(dt_i == 0), stop=(dt_i == DT - 1)), 1 if dt_i == DT - 1 else 0)
            sx_T = cT.n
            sq3 = sq_b[:, 0:DT * W].rearrange("p (d t) -> p d t", t=W)
            for dt_i in range(DT):
                cT.inc(te.matmul(pb[b2][0:1, 0:W], ones1[:], sq3[:, dt_i, :],
                                 start=(dt_i == 0), stop=(dt_i == DT - 1)), 1 if dt_i == DT - 1 else 0)
            sq_T = cT.n
            ve.wait_ge(cT.sem, sq_T)
            cV.inc(ve.tensor_scalar_mul(stat[0:1, 0 * TC:0 * TC + W], pb[b1][0:1, 0:W], 1.0 / D))
            cV.inc(ve.tensor_scalar_mul(stat[0:1, 1 * TC:1 * TC + W], pb[b2][0:1, 0:W], 1.0 / D))
            release_bank(b1, cV)
            release_bank(b2, cV)
            cV.inc(ve.tensor_mul(stat[0:1, 2 * TC:2 * TC + W], stat[0:1, 0 * TC:0 * TC + W], stat[0:1, 0 * TC:0 * TC + W]))
            cV.inc(ve.tensor_sub(stat[0:1, 1 * TC:1 * TC + W], stat[0:1, 1 * TC:1 * TC + W], stat[0:1, 2 * TC:2 * TC + W]))
            cV.inc(ve.tensor_scalar_add(stat[0:1, 1 * TC:1 * TC + W], stat[0:1, 1 * TC:1 * TC + W], EPS))
            cV.inc(ve.reciprocal(stat[0:1, 2 * TC:2 * TC + W], stat[0:1, 1 * TC:1 * TC + W]))
            rec_V = cV.n
            se.wait_ge(cV.sem, rec_V)
            cA.inc(se.activation(out=stat[0:1, 3 * TC:3 * TC + W], in_=stat[0:1, 2 * TC:2 * TC + W], func=AF.Sqrt))
            ve.wait_ge(cA.sem, cA.n)
            cV.inc(ve.tensor_mul(stat[0:1, 4 * TC:4 * TC + W], stat[0:1, 3 * TC:3 * TC + W], stat[0:1, 3 * TC:3 * TC + W]))
            cV.inc(ve.tensor_mul(stat[0:1, 4 * TC:4 * TC + W], stat[0:1, 4 * TC:4 * TC + W], stat[0:1, 1 * TC:1 * TC + W]))
            cV.inc(ve.tensor_scalar(stat[0:1, 4 * TC:4 * TC + W], stat[0:1, 4 * TC:4 * TC + W], -1.0, 3.0,
                                    ALU.mult, ALU.add))
            cV.inc(ve.tensor_mul(stat[0:1, 4 * TC:4 * TC + W], stat[0:1, 4 * TC:4 * TC + W], stat[0:1, 3 * TC:3 * TC + W]))
            cV.inc(ve.tensor_scalar_mul(stat[0:1, 4 * TC:4 * TC + W], stat[0:1, 4 * TC:4 * TC + W], 0.5))
            st_V = cV.n
            te.wait_ge(cV.sem, st_V)
            cT.inc(te.matmul(pb[b1][:, 0:W], onesr[:], stat[0:1, 0 * TC:0 * TC + W],
                             start=True, stop=True), 1)
            cT.inc(te.matmul(pb[b2][:, 0:W], onesr[:], stat[0:1, 4 * TC:4 * TC + W],
                             start=True, stop=True), 1)
            bc_T = cT.n
            ve.wait_ge(cT.sem, bc_T)
            mu3 = pb[b1][:, 0:W].rearrange("p (o t) -> p o t", o=1).broadcast_to([128, DT, W])
            rs3 = pb[b2][:, 0:W].rearrange("p (o t) -> p o t", o=1).broadcast_to([128, DT, W])
            o3 = out_b.rearrange("p (d t) -> p d t", t=W)
            cV.inc(ve.tensor_tensor(o3, i3, mu3, ALU.subtract))
            cV.inc(ve.tensor_tensor(o3, o3, rs3, ALU.mult))
            release_bank(b1, cV)
            release_bank(b2, cV)
            return cV.n

        esA = contextlib.ExitStack()
        sbA = lambda n, s, t: esA.enter_context(nc.sbuf_tensor(n, s, t))
        wi = sbA("wi", [128, 8 * C2], BF16)
        wres = sbA("wres", [128, 8 * C2], BF16)
        cvd = sbA("cvd", [128, 96 * 128], BF16)
        wxp = sbA("wxp", [128, CT * 32], BF16)
        xib_b = sbA("xib_b", [128, CT * TA], BF16)
        cL.inc(ge.dma_start(out=wi[:].rearrange("p (k c) -> p k c", c=C2),
                            in_=W_in.rearrange("(k p) c -> p k c", p=128)[:, :, 0:C2]), 16)
        cL.inc(ge.dma_start(out=wres[:].rearrange("p (k c) -> p k c", c=C2),
                            in_=W_in.rearrange("(k p) c -> p k c", p=128)[:, :, C2:2 * C2]), 16)
        cL.inc(ge.dma_start(out=cvd[:].rearrange("p (i n) -> p i n", n=128),
                            in_=convd.rearrange("i p n -> p i n")), 16)
        cL.inc(ge.dma_start(out=wxp[:].rearrange("p (k c) -> p k c", c=32),
                            in_=W_xproj.rearrange("(k p) c -> p k c", p=128)), 16)
        wload = cL.n
        cS.inc(sy.dma_start(out=A_row[:], in_=A_log.rearrange("(o n) -> o n", o=1)), 16)
        cS.inc(sy.dma_start(out=flag_col[0:1, :], in_=flag), 16)
        cS.inc(sy.dma_start(out=bdt_c[:],
                            in_=b_dt.rearrange("(c p) -> p c", p=128),
                            allow_slow_non_contiguous=True), 16)
        cS.inc(sy.dma_start(out=cvb_c[:],
                            in_=conv_b.rearrange("(c p) -> p c", p=128),
                            allow_slow_non_contiguous=True), 16)
        cS.inc(sy.dma_start(out=Dp_c[:],
                            in_=D_param.rearrange("(c p) -> p c", p=128),
                            allow_slow_non_contiguous=True), 16)
        cS.inc(sy.dma_start(out=bsum_c[:],
                            in_=bs[0].rearrange("(c p) -> p c", p=128),
                            allow_slow_non_contiguous=True), 16)
        cS.inc(sy.dma_start(out=bs_t[:],
                            in_=bs[1].rearrange("(c p) -> p c", p=128),
                            allow_slow_non_contiguous=True), 16)
        cS.inc(sy.dma_start(out=eb_col[:], in_=eb.rearrange("(n o) -> n o", o=1)), 16)
        cS.inc(sy.dma_start(out=sid16[:], in_=id16f), 16)
        cS.inc(sy.dma_start(out=sid128[:], in_=id128f), 16)
        cL.inc(ge.dma_start(out=scent[:].rearrange("p (k c) -> p k c", c=NST),
                            in_=centT.rearrange("(k p) c -> p k c", p=128)), 16)
        setup_S = cS.n
        se.wait_ge(cS.sem, setup_S)
        cA.inc(se.activation(out=A_row[:], in_=A_row[:], func=AF.Exp))
        ve.wait_ge(cA.sem, cA.n)
        ve.wait_ge(cS.sem, setup_S)
        cV.inc(ve.tensor_scalar_mul(A_row[:], A_row[:], -1.0))
        cV.inc(ve.tensor_add(bsum_c[:], bsum_c[:], bs_t[:]))
        cV.inc(ve.tensor_scalar_mul(bsum_c[:], bsum_c[:], 0.5))
        cV.inc(ve.memset(neginf[:], -3.0e38))
        cV.inc(ve.memset(ones1[:], 1.0))
        cV.inc(ve.memset(onesr[:], 1.0))
        cV.inc(ve.memset(onesb[:], 1.0))
        cV.inc(ve.memset(carry[:], 0.0))
        sy.wait_ge(cV.sem, cV.n)
        cS.inc(sy.dma_start(out=A_bc[0:1, :], in_=A_row[:]), 16)
        for k in [1, 2, 4, 8, 16, 32, 64]:
            sy.wait_ge(cS.sem, cS.n)
            cS.inc(sy.dma_start(out=A_bc[k:2 * k, :], in_=A_bc[0:k, :]), 16)
        for k in [1, 2, 4, 8, 16, 32, 64]:
            sy.wait_ge(cS.sem, cS.n)
            cS.inc(sy.dma_start(out=flag_col[k:2 * k, :], in_=flag_col[0:k, :]), 16)
        consts_S = cS.n

        xcb = [sbA(f"xcb{i}", [128, 8 * TA], BF16) for i in range(2)]
        xiraw = sbA("xiraw", [128, CT * (TA + 3)], BF16)
        xr3 = xiraw[:].rearrange("p (c n) -> p c n", n=TA + 3)
        cV.inc(ve.memset(xr3[:, :, 0:3], 0.0))
        halo_v = cV.n
        xcons = [0, 0]
        spill_tok = None
        conv_T = 0
        se.wait_ge(cS.sem, consts_S)
        ve.wait_ge(cS.sem, consts_S)
        for ch in range(NCH):
            bi = ch % 2
            if xcons[bi]:
                ge.wait_ge(cT.sem, xcons[bi])
            cL.inc(ge.dma_start(
                out=xcb[bi][:].rearrange("p (k t) -> p k t", t=TA),
                in_=xT.rearrange("(k p) t -> p k t", p=128)[:, :, ch * TA:(ch + 1) * TA]), 16)
            te.wait_ge(cL.sem, max(cL.n, wload))
            ve.wait_ge(cV.sem, halo_v)
            if spill_tok:
                ve.wait_ge(*spill_tok)
            if ch:
                ve.wait_ge(cT.sem, conv_T)
            for mp in range(CT // 2):
                b = acquire2()
                for mt in (2 * mp, 2 * mp + 1):
                    bb = b + (mt & 1)
                    for kt in range(8):
                        cT.inc(te.matmul(pb[bb][:, 0:TA],
                                         wi[:, kt * C2 + mt * 128: kt * C2 + (mt + 1) * 128],
                                         xcb[bi][:, kt * TA:(kt + 1) * TA],
                                         start=(kt == 0), stop=(kt == 7)), 1 if kt == 7 else 0)
                ve.wait_ge(cT.sem, cT.n)
                cV.inc(ve.tensor_copy(xr3[:, 2 * mp:2 * mp + 2, 3:3 + TA],
                                      pairview(b, TA)))
                release2(b, cV)
            xi_raw_A = cV.n
            te.wait_ge(cV.sem, xi_raw_A)
            for cp in range(CT // 2):
                b = acquire2()
                for ct in (2 * cp, 2 * cp + 1):
                    bb = b + (ct & 1)
                    for tap in range(4):
                        cT.inc(te.matmul(pb[bb][:, 0:TA],
                                         cvd[:, (tap * 16 + ct) * 128:(tap * 16 + ct + 1) * 128],
                                         xr3[:, ct, tap:tap + TA],
                                         start=(tap == 0), stop=False), 0)
                    cT.inc(te.matmul(pb[bb][:, 0:TA],
                                     cvd[:, (64 + ct) * 128:(64 + ct + 1) * 128],
                                     onesb[:, 0:TA], start=False, stop=True), 1)
                se.wait_ge(cT.sem, cT.n)
                cA.inc(se.activation(
                    out=u_b[:, 2 * cp * TA:(2 * cp + 2) * TA].rearrange(
                        "p (b n) -> p b n", n=TA),
                    in_=pairview(b, TA), func=AF.Silu))
                release2(b, cA)
            conv_T = cT.n
            u_A = cA.n
            ve.wait_ge(cT.sem, conv_T)
            cV.inc(ve.tensor_copy(xr3[:, :, 0:3], xr3[:, :, TA:TA + 3]))
            halo_v = cV.n
            ve.wait_ge(cA.sem, u_A)
            if spill_tok:
                ve.wait_ge(*spill_tok)
            cV.inc(ve.tensor_copy(xib_b[:], u_b[:]))
            xib_V = cV.n
            b = acquire_bank()
            te.wait_ge(cV.sem, xib_V)
            for kt in range(CT):
                cT.inc(te.matmul(pb[b][0:32, 0:TA], wxp[:, kt * 32:(kt + 1) * 32],
                                 xib_b[:, kt * TA:(kt + 1) * TA],
                                 start=(kt == 0), stop=(kt == CT - 1)), 1 if kt == CT - 1 else 0)
            se.wait_ge(cT.sem, cT.n)
            cA.inc(se.activation(out=BC_buf[:, ch * TA:(ch + 1) * TA], in_=pb[b][0:32, 0:TA],
                                 func=AF.Copy))
            release_bank(b, cA)
            sres_A = 0
            if ch >= NRE:
                for mp in range(CT // 2):
                    b = acquire2()
                    for mt in (2 * mp, 2 * mp + 1):
                        bb = b + (mt & 1)
                        for kt in range(8):
                            cT.inc(te.matmul(pb[bb][:, 0:TA],
                                             wres[:, kt * C2 + mt * 128: kt * C2 + (mt + 1) * 128],
                                             xcb[bi][:, kt * TA:(kt + 1) * TA],
                                             start=(kt == 0), stop=(kt == 7)), 1 if kt == 7 else 0)
                    se.wait_ge(cT.sem, cT.n)
                    cA.inc(se.activation(
                        out=sres_b[:, 2 * mp * TA:(2 * mp + 2) * TA].rearrange(
                            "p (b n) -> p b n", n=TA),
                        in_=pairview(b, TA), func=AF.Silu))
                    release2(b, cA)
                sres_A = cA.n
            xcons[bi] = cT.n
            rA.begin()
            syE.wait_ge(cA.sem, u_A)
            rA.dma(syE, out=u_sp[ch], in_=u_b[:])
            syE.wait_ge(cV.sem, xib_V)
            rA.dma(syE, out=xib_sp[ch], in_=xib_b[:])
            if ch >= NRE:
                syE.wait_ge(cA.sem, sres_A)
                rA.dma(syE, out=sres_sp[ch], in_=sres_b[:])
            spill_tok = rA.tok()

        sy.wait_ge(cA.sem, cA.n)
        cS.inc(sy.dma_start(out=bc_hbm, in_=BC_buf[:]), 16)
        ph_barrier()
        esA.close()
        if phases < 2:
            return

        esB = contextlib.ExitStack()
        sbB = lambda n, s, t: esB.enter_context(nc.sbuf_tensor(n, s, t))
        TAp = TA + 1
        dA_b = [sbB(f"dAb{i}", [128, CT * TAp], F32) for i in range(2)]
        dBu_b = sbB("dBu", [128, CT * TAp], F32)
        h_b = sbB("h_b", [128, CT * TAp], F32)
        y_acc = sbB("y_acc", [128, CT * TA], F32)
        Bb_b = sbB("Bb_b", [128, NST * TA], F32)
        Cb_b = sbB("Cb_b", [128, NST * TA], F32)
        Bbc = Bb_b[:]
        Cbc = Cb_b[:]

        def bchain(eng, buf, r0, win):
            rB.dma(eng,
                   out=buf[:].rearrange("p (n t) -> p n t", n=NST),
                   in_=bc_hbm[r0:r0 + NST, win * TA:(win + 1) * TA][None]
                   .broadcast_to([128, NST, TA]))
        g_b = sbB("g_b", [128, CT * TA], BF16)
        dA3 = [dA_b[i][:].rearrange("p (c n) -> p c n", n=TAp) for i in range(2)]
        dBu3 = dBu_b[:].rearrange("p (c n) -> p c n", n=TAp)
        h3 = h_b[:].rearrange("p (c n) -> p c n", n=TAp)
        d3 = d_b[:].rearrange("p (c t) -> p c t", t=TA)
        du3 = du_b[:].rearrange("p (c t) -> p c t", t=TA)
        y3 = y_acc[:].rearrange("p (c t) -> p c t", t=TA)
        ve.wait_ge(cS.sem, consts_S)
        for i in range(2):
            cV.inc(ve.memset(dA3[i][:, :, 0:1], 0.0))
        pad_V = cV.n
        pool_reset()
        te.wait_ge(cS.sem, consts_S)

        wdt_sl = Cb_b[:].bitcast(BF16)
        xcv = y_acc[:].bitcast(BF16)
        xib2 = xcv[:, 0:CT * TA]
        e1p = g_b[:].bitcast(F32)
        d2_b = sres_b
        cL.inc(ge.dma_start(
            out=xcv[:, CT * TA:CT * TA + 2048].rearrange("p (i n) -> p i n", n=128),
            in_=convd.rearrange("i p n -> p i n")[:, 80:96, :]), 16)
        cvd2_L = cL.n
        WdtR = W_dt.rearrange("(k p) c -> p k c", p=128)
        a2st = dict(mmend=0, slfree=[0, 0], spar=0, spill=0, lastA=0)
        a2_spill = [0] * NCH
        a2evacs = []

        def a2_group(k, mp):
            if mp == 0:
                sy.wait_ge(cT.sem, a2st['mmend'])
                cS.inc(sy.dma_start(out=xib2, in_=xib_sp[k]), 16)
                a2st['xload'] = cS.n
            sl = a2st['spar']
            a2st['spar'] ^= 1
            slab = wdt_sl[:, sl * 4096:(sl + 1) * 4096]
            rSl.begin()
            geE.wait_ge(cT.sem, a2st['slfree'][sl])
            rSl.dma(geE,
                    out=slab.rearrange("p (j c) -> p j c", c=256),
                    in_=WdtR[:, :, 2 * mp * 128:(2 * mp + 2) * 128])
            te.wait_ge(*rSl.tok())
            if mp == 0:
                te.wait_ge(cS.sem, a2st['xload'])
                te.wait_ge(cL.sem, cvd2_L)
            b = acquire2()
            for i in (0, 1):
                mt = 2 * mp + i
                bb = b + i
                for kt in range(CT):
                    cT.inc(te.matmul(pb[bb][:, 0:TA],
                                     slab[:, kt * 256 + i * 128: kt * 256 + (i + 1) * 128],
                                     xib2[:, kt * TA:(kt + 1) * TA],
                                     start=(kt == 0), stop=False), 0)
                cT.inc(te.matmul(pb[bb][:, 0:TA],
                                 xcv[:, CT * TA + mt * 128:CT * TA + (mt + 1) * 128],
                                 onesb[:, 0:TA], start=False, stop=True), 1)
            a2st['slfree'][sl] = cT.n
            if mp == CT // 2 - 1:
                a2st['mmend'] = cT.n
            a2evacs.append((b, k, mp, cT.n))

        def a2_evac(b, k, mp, mmT):
            se.wait_ge(cT.sem, mmT)
            cA.inc(se.activation(
                out=e1p[:, 0:2 * TA].rearrange("p (b n) -> p b n", n=TA),
                in_=pairview(b, TA), func=AF.Exp))
            if mp == 0 and a2st['spill']:
                se.wait_ge(cL.sem, a2st['spill'])
            cA.inc(se.activation(
                out=d2_b[:, 2 * mp * TA:(2 * mp + 2) * TA].rearrange(
                    "p (b n) -> p b n", n=TA),
                in_=e1p[:, 0:2 * TA].rearrange("p (b n) -> p b n", n=TA),
                func=AF.Ln, bias=1.0))
            release2(b, cA)
            a2st['lastA'] = cA.n
            if mp == CT // 2 - 1:
                ge.wait_ge(cA.sem, cA.n)
                cL.inc(ge.dma_start(out=d_sp[k], in_=d2_b[:]), 16)
                a2st['spill'] = cL.n
                a2_spill[k] = cL.n

        for mp in range(CT // 2):
            if len(a2evacs) >= 4:
                a2_evac(*a2evacs.pop(0))
            a2_group(0, mp)
        while a2evacs:
            a2_evac(*a2evacs.pop(0))
        rB.begin()
        syE.wait_ge(cL.sem, cL.n)
        rB.dma(syE, out=d_b[:], in_=d_sp[0])
        bchain(syE, Bb_b, 0, 0)
        ev_next = rB.tok()
        rDu.begin()
        geE.wait_ge(cL.sem, cL.n)
        rDu.dma(geE, out=du_b[:], in_=u_sp[0])
        du_next = rDu.tok()
        for mp in range(CT // 2):
            if len(a2evacs) >= 4:
                a2_evac(*a2evacs.pop(0))
            a2_group(1, mp)
        while a2evacs:
            a2_evac(*a2evacs.pop(0))

        zmm_T = 0
        prev_scan = 0
        dfree_A = 0
        dufree_V = 0
        dumult_V = 0
        gspill_L = 0
        yld_next = None
        for ch in range(NCH):
            a2_done = 0
            real = ch >= NRE
            ev_cur = ev_next
            if ch <= NRE - 2:
                a2_sched = [(2 * ch + 2, mp) for mp in range(CT // 2)] + \
                           [(2 * ch + 3, mp) for mp in range(CT // 2)]
                a2_target = [0, 0, 0, 0, 1, 2, 3, 4, 5, 6, 7, 8, 10, 12, 14, 15]
            else:
                a2_sched = None
            if ch == NRE:
                zmm_T = a2st['mmend']
                ve.wait_ge(cS.sem, consts_S)
                ve.wait_ge(cA.sem, cA.n)
                cV.inc(ve.tensor_scalar_mul(carry[:], carry[:], flag_col[:]))
                prev_scan = cV.n
            du_cur = du_next
            if real:
                cb_cur = cb_next
            ve.wait_ge(*ev_cur)
            ve.wait_ge(*du_cur)
            cV.inc(ve.tensor_mul(du_b[:], d_b[:], du_b[:]))
            dumult_V = cV.n
            se.wait_ge(*ev_cur)
            se.wait_ge(cV.sem, pad_V)
            if ch + 1 < NCH:
                rB.begin()
            dA_used = [0, 0]

            def emit_dA(n):
                pi = n % 2
                if dA_used[pi]:
                    se.wait_ge(cV.sem, dA_used[pi])
                cA.inc(se.activation(out=dA3[pi][:, :, 1:TAp], in_=d3,
                                     func=AF.Exp, scale=A_bc[:, n:n + 1]))
                return cA.n

            def emit_pad(n):
                se.wait_ge(cV.sem, prev_scan)
                cA.inc(se.activation(
                    out=dBu3[:, :, 0:1],
                    in_=carry[:, n * CT:(n + 1) * CT].rearrange("p (c o) -> p c o", o=1),
                    func=AF.Copy))
                return cA.n

            emit_dA(0)
            actn = emit_pad(0)
            for n in range(NST):
                if a2_sched is not None:
                    while a2_done < a2_target[n]:
                        a2_evac(*a2evacs.pop(0))
                        a2_done += 1
                    a2_group(*a2_sched[n])
                pi = n % 2
                bview = Bbc[:, n * TA:(n + 1) * TA].rearrange(
                    "p (o t) -> p o t", o=1).broadcast_to([128, CT, TA])
                cV.inc(ve.tensor_tensor(dBu3[:, :, 1:TAp], du3, bview, ALU.mult))
                if n == NST - 1:
                    dufree_V = cV.n
                    if ch + 1 < NCH:
                        syE.wait_ge(cL.sem, a2_spill[ch + 1])
                        syE.wait_ge(cA.sem, dfree_A)
                        syE.wait_ge(cV.sem, dumult_V)
                        rB.dma(syE, out=d_b[:], in_=d_sp[ch + 1])
                        syE.wait_ge(cV.sem, dufree_V)
                        bchain(syE, Bb_b, 0, ch + 1)
                        rDu.begin()
                        geE.wait_ge(cV.sem, dufree_V)
                        rDu.dma(geE, out=du_b[:], in_=u_sp[ch + 1])
                        du_next = rDu.tok()
                ve.wait_ge(cA.sem, actn)
                cV.inc(ve.tensor_tensor_scan(out=h_b[:], data0=dA_b[pi][:],
                                             data1=dBu_b[:], initial=0.0,
                                             op0=ALU.mult, op1=ALU.add))
                scan_V = cV.n
                prev_scan = scan_V
                dA_used[pi] = cV.n
                if n + 1 < NST:
                    dn = emit_dA(n + 1)
                    if n + 1 == NST - 1:
                        dfree_A = dn
                se.wait_ge(cV.sem, scan_V)
                cA.inc(se.activation(
                    out=carry[:, n * CT:(n + 1) * CT].rearrange("p (c o) -> p c o", o=1),
                    in_=h3[:, :, TA:TAp], func=AF.Copy))
                if n + 1 < NST:
                    actn = emit_pad(n + 1)
                if real:
                    cview = Cbc[:, n * TA:(n + 1) * TA].rearrange(
                        "p (o t) -> p o t", o=1).broadcast_to([128, CT, TA])
                    if zmm_T:
                        ve.wait_ge(cT.sem, zmm_T)
                    if n == 0:
                        ve.wait_ge(*cb_cur)
                    cV.inc(ve.tensor_tensor(y3, h3[:, :, 1:TAp], cview, ALU.mult))
                    if ch == NRE and n == 0:
                        te.wait_ge(cA.sem, a2st['lastA'])
                    te.wait_ge(cV.sem, cV.n)
                    for zb in range(8):
                        cT.inc(te.matmul(
                            PP[:, zb * 512:(zb + 1) * 512],
                            sid128[:], y_acc[:, zb * 512:(zb + 1) * 512],
                            start=(n == 0), stop=(n == NST - 1)),
                               1 if zb == 7 else 0)
                    zmm_T = cT.n
            while a2evacs:
                a2_evac(*a2evacs.pop(0))
            if ch + 1 < NCH:
                if ch + 1 == NRE:
                    rY.begin()
                    syE.wait_ge(cL.sem, a2st['spill'])
                    rY.dma(syE, out=sres_b[:], in_=sres_sp[NRE])
                    rY.dma(syE, out=u_b[:], in_=u_sp[NRE])
                    yld_next = rY.tok()
                if ch + 1 >= NRE:
                    rC.begin()
                    if ch + 1 == NRE:
                        geE.wait_ge(cT.sem, a2st['mmend'])
                    geE.wait_ge(cV.sem, cV.n if real else dufree_V)
                    rC.dma(geE,
                           out=Cb_b[:].rearrange("p (n t) -> p n t", n=NST),
                           in_=bc_hbm[NST:2 * NST,
                                      (ch + 1) * TA:(ch + 2) * TA][None]
                           .broadcast_to([128, NST, TA]))
                    cb_next = rC.tok()
            if real:
                yld_cur = yld_next
                ve.wait_ge(cT.sem, zmm_T)
                ve.wait_ge(*yld_cur)
                for ct in range(CT):
                    cV.inc(ve.scalar_tensor_tensor(
                        out=y_acc[:, ct * TA:(ct + 1) * TA],
                        in0=u_b[:, ct * TA:(ct + 1) * TA],
                        scalar=Dp_c[:, ct:ct + 1],
                        in1=PP[:, ct * TA:(ct + 1) * TA],
                        op0=ALU.mult, op1=ALU.add))
                if ch == NRE:
                    ve.wait_ge(cA.sem, a2st['lastA'])
                ve.wait_ge(cL.sem, gspill_L)
                cV.inc(ve.tensor_mul(g_b[:], y_acc[:], sres_b[:]))
                ge.wait_ge(cV.sem, cV.n)
                cL.inc(ge.dma_start(out=g_sp[ch], in_=g_b[:]), 16)
                gspill_L = cL.n
                if ch + 1 < NCH:
                    rY.begin()
                    syE.wait_ge(cV.sem, cV.n)
                    rY.dma(syE, out=u_b[:], in_=u_sp[ch + 1])
                    rY.dma(syE, out=sres_b[:], in_=sres_sp[ch + 1])
                    yld_next = rY.tok()
            if ch + 1 < NCH:
                ev_next = rB.tok()

        ph_barrier()
        esB.close()
        if phases < 4:
            return

        u2_b = sb("u2_b", [128, DT * TC], F32)
        u2b_b = sb("u2b_b", [128, DT * TC], BF16)
        gates_r = sb("gates_r", [16, TH], F32)
        gates_rb = sb("gates_rb", [16, TH], BF16)
        stat = sb("stat", [1, 5 * TC], F32)
        mu_bc = sb("mu_bc", [128, TC], F32)
        rs_bc = sb("rs_bc", [128, TC], F32)
        esC = contextlib.ExitStack()
        sbC = lambda n, s, t: esC.enter_context(nc.sbuf_tensor(n, s, t))
        wout = sbC("wout", [128, CT * D], BF16)
        gbig = sbC("gbig", [128, CT * TC], BF16)
        resid2 = sbC("resid2", [128, DT * TC], F32)
        sc_b = sbC("sc_b", [16, TC], F32)
        sT_b = sbC("sT_b", [128, 4 * NST], F32)
        m_b = sbC("m_b", [128, 8], F32)
        msk_b = sbC("msk_b", [128, NST], U8)
        gT_b = sbC("gT_b", [128, 4 * NST], F32)
        cL.inc(ge.dma_start(out=wout[:].rearrange("p (k c) -> p k c", c=D),
                            in_=W_out.rearrange("(k p) c -> p k c", p=128)), 16)
        wout_L = cL.n
        te.wait_ge(cS.sem, consts_S)
        ve.wait_ge(cS.sem, consts_S)
        xc_par = [d_b, du_b]
        gb_par = [gbig[:], sres_b[:].bitcast(BF16)[:, 0:CT * TC]]
        rs_par = [u_b, resid2]
        ldc_tok = [None, None]
        mmend_par = [0, 0]
        evacV_par = [0, 0]

        def c_loads(cc):
            par = cc % 2
            rB.begin()
            syE.wait_ge(cT.sem, mmend_par[par])
            gv3 = gb_par[par].rearrange("p (c t) -> p c t", t=TC)
            rB.dma(syE, out=gv3[:, :, 0:TA],
                   in_=g_sp[NRE + 2 * cc].rearrange("p (c t) -> p c t", t=TA))
            rB.dma(syE, out=gv3[:, :, TA:TC],
                   in_=g_sp[NRE + 2 * cc + 1].rearrange("p (c t) -> p c t", t=TA))
            syE.wait_ge(cV.sem, evacV_par[par])
            rB.dma(syE, out=xc_par[par][:].rearrange("p (k t) -> p k t", t=TC),
                   in_=xT.rearrange("(k p) t -> p k t", p=128)[
                       :, :, TH + cc * TC:TH + (cc + 1) * TC])
            ldc_tok[par] = rB.tok()

        c_loads(0)
        for cc in range(NCC):
            par = cc % 2
            if cc + 1 < NCC:
                c_loads(cc + 1)
            te.wait_ge(*ldc_tok[par])
            te.wait_ge(cL.sem, wout_L)
            ve.wait_ge(*ldc_tok[par])
            for mt in range(DT):
                b = acquire_bank()
                for kt in range(CT):
                    cT.inc(te.matmul(pb[b][:, 0:TC],
                                     wout[:, kt * D + mt * 128: kt * D + (mt + 1) * 128],
                                     gb_par[par][:, kt * TC:(kt + 1) * TC],
                                     start=(kt == 0), stop=(kt == CT - 1)), 1 if kt == CT - 1 else 0)
                ve.wait_ge(cT.sem, cT.n)
                cV.inc(ve.tensor_add(rs_par[par][:, mt * TC:(mt + 1) * TC], pb[b][:, 0:TC],
                                     xc_par[par][:, mt * TC:(mt + 1) * TC]))
                release_bank(b, cV)
            mmend_par[par] = cT.n
            evacV_par[par] = cV.n
            emit_ln(rs_par[par][:], u2_b[:], TC, u2_b[:], stat, mu_bc, rs_bc)
            ve.wait_ge(cT.sem, cT.n)
            cV.inc(ve.tensor_copy(u2b_b[:], u2_b[:]))
            u2_V = cV.n
            sy.wait_ge(cV.sem, u2_V)
            cS.inc(sy.dma_start(out=u2_sp[cc], in_=u2_b[:]), 16)
            b = acquire_bank()
            te.wait_ge(cV.sem, u2_V)
            for kt in range(DT):
                cT.inc(te.matmul(pb[b][0:16, 0:TC], scent[:, kt * NST:(kt + 1) * NST],
                                 u2b_b[:].rearrange("p (k t) -> p k t", t=TC)[:, kt, :],
                                 start=(kt == 0), stop=(kt == DT - 1)), 1 if kt == DT - 1 else 0)
            sc_T = cT.n
            ve.wait_ge(cT.sem, sc_T)
            cV.inc(ve.tensor_scalar(sc_b[0:16, 0:TC], pb[b][0:16, 0:TC],
                                    eb_col[:], None, ALU.add))
            release_bank(b, cV)
            raw_A = cV.n
            bT = acquire_bank()
            te.wait_ge(cV.sem, raw_A)
            for tb in range(4):
                cT.inc(te.transpose(pb[bT][:, tb * NST:(tb + 1) * NST],
                                    sc_b[0:16, tb * 128:(tb + 1) * 128], sid16[:]),
                       1 if tb == 3 else 0)
            ve.wait_ge(cT.sem, cT.n)
            cV.inc(ve.tensor_copy(sT_b[:], pb[bT][:, 0:4 * NST]))
            release_bank(bT, cV)
            sT3 = sT_b[:].rearrange("p (b n) -> p b n", n=NST)
            for tb in range(4):
                cV.inc(ve.reduce_max(m_b[:, tb:tb + 1], sT3[:, tb, :], axis=AX.X))
                cV.inc(ve.tensor_scalar(sT3[:, tb, :], sT3[:, tb, :],
                                        m_b[:, tb:tb + 1], None, ALU.subtract))
            sm_V = cV.n
            se.wait_ge(cV.sem, sm_V)
            cA.inc(se.activation(out=sT_b[:], in_=sT_b[:], func=AF.Exp))
            exp_A = cA.n
            ve.wait_ge(cA.sem, exp_A)
            for tb in range(4):
                cV.inc(ve.reduce_sum(m_b[:, 4 + tb:5 + tb], sT3[:, tb, :], axis=AX.X))
                cV.inc(ve.reciprocal(m_b[:, 4 + tb:5 + tb], m_b[:, 4 + tb:5 + tb]))
                cV.inc(ve.tensor_scalar(sT3[:, tb, :], sT3[:, tb, :],
                                        m_b[:, 4 + tb:5 + tb], None, ALU.mult))
                cV.inc(ve.reduce_max(m_b[:, tb:tb + 1], sT3[:, tb, :], axis=AX.X))
                cV.inc(ve.tensor_scalar(msk_b[:], sT3[:, tb, :], m_b[:, tb:tb + 1],
                                        None, ALU.is_ge))
                cV.inc(ve.tensor_copy(gT_b[:, tb * NST:(tb + 1) * NST], sT3[:, tb, :]))
                cV.inc(ve.copy_predicated(gT_b[:, tb * NST:(tb + 1) * NST], msk_b[:],
                                          neginf[:]))
                cV.inc(ve.reduce_max(m_b[:, tb:tb + 1],
                                     gT_b[:].rearrange("p (b n) -> p b n", n=NST)[:, tb, :],
                                     axis=AX.X))
                cV.inc(ve.tensor_scalar(gT_b[:, tb * NST:(tb + 1) * NST], sT3[:, tb, :],
                                        m_b[:, tb:tb + 1], None, ALU.is_ge))
                cV.inc(ve.tensor_mul(gT_b[:, tb * NST:(tb + 1) * NST],
                                     gT_b[:, tb * NST:(tb + 1) * NST], sT3[:, tb, :]))
            g2_V = cV.n
            bG = acquire_bank()
            te.wait_ge(cV.sem, g2_V)
            for tb in range(4):
                cT.inc(te.transpose(pb[bG][0:16, tb * 128:(tb + 1) * 128],
                                    gT_b[:, tb * NST:(tb + 1) * NST], sid128[:]),
                       1 if tb == 3 else 0)
            se.wait_ge(cT.sem, cT.n)
            cA.inc(se.activation(out=gates_r[:, cc * TC:(cc + 1) * TC],
                                 in_=pb[bG][0:16, 0:TC], func=AF.Copy))
            release_bank(bG, cA)

        cA.inc(se.activation(out=gates_rb[:], in_=gates_r[:], func=AF.Copy))
        sy.wait_ge(cA.sem, cA.n)
        cS.inc(sy.dma_start(out=gates_hbm, in_=gates_r[:]), 16)
        ph_barrier()
        if phases < 5:
            esC.close()
            return

        esC.close()
        esD = contextlib.ExitStack()
        sbD = lambda n, s, t: esD.enter_context(nc.sbuf_tensor(n, s, t))
        wr_b = [sbD(f"wr{i}", [128, DT * D], BF16) for i in range(2)]
        ue_b = [sbD(f"ue{i}", [128, DT * TC], BF16) for i in range(2)]
        gbc2_b = [sbD("gbc_b", [128, 4 * TC], F32),
                  u2b_b[:].bitcast(F32)[:, 0:4 * TC]]
        uh_b = sbD("uh_b", [128, DT * TC], BF16)
        ln2_b = u_b
        out_b = sres_b
        wbr = sbD("wbr", [16, DT * 128], BF16)
        cL.inc(ge.dma_start(out=wbr[:], in_=br), 16)
        wd_L = cL.n
        wr_loaded = [0, 0]
        wr_freeT = [0, 0]
        ue_V = [0, 0]
        ue_freeT = [0, 0]
        gbc_S = [0, 0]
        gbc_cons = [0, 0]

        def gbc_chain(tch, grp):
            par = grp % 2
            buf = gbc2_b[par]
            if gbc_cons[par]:
                sy.wait_ge(cV.sem, gbc_cons[par])
            cS.inc(sy.dma_start(
                out=buf[0:1, :].rearrange("o (n t) -> o n t", n=4),
                in_=gates_hbm[4 * grp:4 * grp + 4,
                              tch * TC:(tch + 1) * TC][None]), 16)
            for k in [1, 2, 4, 8, 16, 32, 64]:
                sy.wait_ge(cS.sem, cS.n)
                cS.inc(sy.dma_start(out=buf[k:2 * k, :], in_=buf[0:k, :]), 16)
            gbc_S[par] = cS.n

        for tch in range(NCC):
            cS.inc(sy.dma_start(out=u2_b[:], in_=u2_sp[tch]), 16)
            u2_S = cS.n
            ve.wait_ge(cS.sem, u2_S)
            ve.wait_ge(cT.sem, cT.n)
            cV.inc(ve.tensor_scalar_mul(uh_b[:], u2_b[:], 0.5))
            base_V = cV.n
            experts = list(range(NEXP)) + [NEXP, NEXP + 1]

            def load_wr(e, wi_):
                if wr_freeT[wi_]:
                    ge.wait_ge(cT.sem, wr_freeT[wi_])
                src_ap = (WrT[e] if e < NEXP else WsT[e - NEXP]).rearrange(
                    "(k p) c -> p k c", p=128)
                cL.inc(ge.dma_start(
                    out=wr_b[wi_][:].rearrange("p (k c) -> p k c", c=D),
                    in_=src_ap), 16)
                wr_loaded[wi_] = cL.n

            load_wr(experts[0], 0)
            load_wr(experts[1], 1)
            gbc_chain(tch, 0)
            gbc_chain(tch, 1)
            banks_d = [acquire_bank() for _ in range(DT)]
            te.wait_ge(cS.sem, u2_S)
            te.wait_ge(cL.sem, wd_L)
            for dt_i in range(DT):
                cT.inc(te.matmul(pb[banks_d[dt_i]][:, 0:TC],
                                 wbr[:, dt_i * 128:(dt_i + 1) * 128],
                                 gates_rb[:, tch * TC:(tch + 1) * TC],
                                 start=True, stop=False),
                       1 if dt_i == DT - 1 else 0)
            for ei, e in enumerate(experts):
                wi_ = ei % 2
                if e < NEXP:
                    par = (e // 4) % 2
                    gv = gbc2_b[par][:, (e % 4) * TC:(e % 4 + 1) * TC].rearrange(
                        "p (o t) -> p o t", o=1).broadcast_to([128, DT, TC])
                    ve.wait_ge(cV.sem, base_V)
                    ve.wait_ge(cS.sem, gbc_S[par])
                    if ue_freeT[wi_]:
                        ve.wait_ge(cT.sem, ue_freeT[wi_])
                    cV.inc(ve.scalar_tensor_tensor(
                        out=ue_b[wi_][:].rearrange("p (k t) -> p k t", t=TC),
                        in0=u2_b[:].rearrange("p (k t) -> p k t", t=TC),
                        scalar=ones1[:], in1=gv, op0=ALU.mult, op1=ALU.mult))
                    ue_V[wi_] = cV.n
                    if e % 4 == 3:
                        gbc_cons[par] = cV.n
                        if e + 8 < NEXP:
                            gbc_chain(tch, e // 4 + 2)
                    mov = ue_b[wi_]
                    te.wait_ge(cV.sem, ue_V[wi_])
                else:
                    mov = uh_b
                    te.wait_ge(cV.sem, base_V)
                te.wait_ge(cL.sem, wr_loaded[wi_])
                last = (ei == len(experts) - 1)
                for dt_i in range(DT):
                    for kt in range(DT):
                        cT.inc(te.matmul(
                            pb[banks_d[dt_i]][:, 0:TC],
                            wr_b[wi_][:, kt * D + dt_i * 128: kt * D + (dt_i + 1) * 128],
                            mov[:, kt * TC:(kt + 1) * TC],
                            start=False, stop=(last and kt == DT - 1)),
                           1 if (dt_i == DT - 1 and kt == DT - 1) else 0)
                ue_freeT[wi_] = cT.n
                wr_freeT[wi_] = cT.n
                if ei + 2 < len(experts):
                    load_wr(experts[ei + 2], (ei + 2) % 2)
            moe_T = cT.n
            ve.wait_ge(cT.sem, moe_T)
            for dt_i in range(DT):
                cV.inc(ve.scalar_tensor_tensor(
                    out=ln2_b[:, dt_i * TC:(dt_i + 1) * TC],
                    in0=pb[banks_d[dt_i]][:, 0:TC], scalar=bsum_c[:, dt_i:dt_i + 1],
                    in1=u2_b[:, dt_i * TC:(dt_i + 1) * TC], op0=ALU.add, op1=ALU.add))
                release_bank(banks_d[dt_i], cV)
            cV.inc(ve.tensor_add(ln2_b[:], ln2_b[:], u2_b[:]))
            sy.wait_ge(cV.sem, cV.n)
            cS.inc(sy.dma_start(out=ffnin_sp[tch], in_=ln2_b[:]), 16)
            emit_ln(ln2_b[:], out_b[:], TC, out_b[:], stat, mu_bc, rs_bc)
            out_V = cV.n
            sy.wait_ge(cV.sem, out_V)
            cS.inc(sy.dma_start(
                out=oT.rearrange("(k p) t -> p k t", p=128)[:, :, tch * TC:(tch + 1) * TC],
                in_=out_b[:].rearrange("p (k t) -> p k t", t=TC)), 16)
            sy.wait_ge(cV.sem, cV.n)

        sy.wait_ge(cS.sem, cS.n)
        ph_barrier()
        esD.close()

    blk.sync(emit)
    es.close()
    return nc


def kernel(L0_attn_x, L0_attn_W_in, L0_attn_conv_w, L0_attn_conv_b,
           L0_attn_W_xproj, L0_attn_W_dt, L0_attn_b_dt, L0_attn_A_log,
           L0_attn_D_param, L0_attn_W_out, L0_ffn_centroids, L0_ffn_expert_biases,
           L0_ffn_Wr, L0_ffn_br, L0_ffn_Ws, L0_ffn_bs, _debug=False, _phases=5):
    x = np.ascontiguousarray(np.asarray(L0_attn_x))
    B = x.shape[0]
    conv_w = np.asarray(L0_attn_conv_w).reshape(C2, 4)
    convd = np.zeros((96, 128, 128), np.float32)
    idx = np.arange(128)
    for tap in range(4):
        for ct in range(CT):
            convd[tap * 16 + ct, idx, idx] = conv_w[ct * 128:(ct + 1) * 128, tap]
    cb = np.asarray(L0_attn_conv_b).reshape(CT, 128)
    bd = np.asarray(L0_attn_b_dt).reshape(CT, 128)
    for ct in range(CT):
        convd[64 + ct, idx, idx] = cb[ct]
        convd[80 + ct, idx, idx] = bd[ct]
    WrT = np.ascontiguousarray(np.transpose(np.asarray(L0_ffn_Wr), (0, 2, 1)))
    WsT = np.ascontiguousarray(np.transpose(np.asarray(L0_ffn_Ws), (0, 2, 1)))
    centT = np.ascontiguousarray(np.asarray(L0_ffn_centroids).T)
    id16f = np.eye(16, dtype=np.float32)

    common = dict(
        W_in=np.ascontiguousarray(L0_attn_W_in), convd=convd,
        conv_b=np.ascontiguousarray(L0_attn_conv_b),
        W_xproj=np.ascontiguousarray(L0_attn_W_xproj),
        W_dt=np.ascontiguousarray(L0_attn_W_dt),
        b_dt=np.ascontiguousarray(L0_attn_b_dt),
        A_log=np.ascontiguousarray(L0_attn_A_log),
        D_param=np.ascontiguousarray(L0_attn_D_param),
        W_out=np.ascontiguousarray(L0_attn_W_out),
        centT=centT, eb=np.ascontiguousarray(L0_ffn_expert_biases),
        WrT=WrT, br=np.ascontiguousarray(L0_ffn_br), WsT=WsT,
        bs=np.ascontiguousarray(L0_ffn_bs), id16f=id16f, id128f=np.eye(128, dtype=np.float32))

    in_maps = []
    for c in range(8):
        b, j = c // 2, c % 2
        if j == 0:
            xv = np.concatenate([np.zeros((TH, D), np.float32), x[b, 0:TH]], 0)
            fl = np.zeros((1, 1), np.float32)
        else:
            xv = np.concatenate([x[b, 0:TH], x[b, TH:]], 0)
            fl = np.ones((1, 1), np.float32)
        m = dict(common)
        m["xT"] = np.ascontiguousarray(xv.T)
        m["flag"] = fl
        in_maps.append(m)

    nc = build_nc(debug=_debug, phases=_phases)
    res = run_bass_kernel_spmd(nc, in_maps, core_ids=list(range(8)))
    out = np.empty((B, TV, D), np.float32)
    for c in range(8):
        b, j = c // 2, c % 2
        out[b, j * TH:(j + 1) * TH, :] = res.results[c]["oT"].T
    if _debug:
        return out, res
    return out


# version 19
# speedup vs baseline: 1.0070x; 1.0070x over previous
import contextlib
import os
import numpy as np
import ml_dtypes
import concourse.bass as bass
import concourse.mybir as mybir
from concourse.bass_utils import run_bass_kernel_spmd

F32 = mybir.dt.float32
BF16 = mybir.dt.bfloat16
U8 = mybir.dt.uint8
ALU = mybir.AluOpType
AF = mybir.ActivationFunctionType
AX = mybir.AxisListType

D = 1024
C2 = 2048
NST = 16
TV = 4096
TH = 2048
TA = 256
NCH = TV // TA
NRE = TH // TA
TC = 512
NCC = TH // TC
CT = C2 // 128
DT = D // 128
NEXP = 16
EPS = 1e-5


class _Chained:
    def then_inc(self, *a):
        return self


_CHAINED = _Chained()


class Ctr:
    def __init__(self, es, nc, name):
        self.sem = es.enter_context(nc.semaphore(name))
        self.n = 0

    def inc(self, inst, k=1):
        if inst is _CHAINED:
            return inst
        if k:
            inst.then_inc(self.sem, k)
            self.n += k
        return inst


_CHAIN_OPS = {
    "tensor_copy", "tensor_tensor", "tensor_add", "tensor_sub", "tensor_mul",
    "tensor_max", "tensor_scalar", "tensor_single_scalar", "tensor_scalar_add",
    "tensor_scalar_sub", "tensor_scalar_mul", "tensor_scalar_max",
    "tensor_scalar_min", "tensor_relu", "tensor_reduce", "reduce_max",
    "reduce_sum", "scalar_tensor_tensor", "tensor_tensor_scan", "memset",
    "select", "copy_predicated", "reciprocal", "activation", "copy", "mul", "add", "square",
    "sqrt", "sign",
}


class DmaQ:

    def __init__(self, eng, ctr):
        self._e = eng
        self._c = ctr

    def wait_ge(self, sem, v):
        return self._e.wait_ge(sem, v)

    def dma_start(self, *a, **k):
        self._e.wait_ge(self._c.sem, self._c.n)
        inst = self._e.dma_start(*a, **k)
        inst.then_inc(self._c.sem, 16)
        self._c.n += 16
        return _CHAINED

    def __getattr__(self, name):
        return getattr(self._e, name)


class Rot:

    def __init__(self, es, nc, name, depth=4):
        self.sems = [es.enter_context(nc.semaphore(f"{name}{i}"))
                     for i in range(depth)]
        self.ns = [0] * depth
        self.i = -1

    def begin(self):
        self.i = (self.i + 1) % len(self.sems)

    def dma(self, eng, **k):
        eng.dma_start(**k).then_inc(self.sems[self.i], 16)
        self.ns[self.i] += 16

    def tok(self):
        return (self.sems[self.i], self.ns[self.i])


class Chain:

    def __init__(self, eng, ctr):
        self._e = eng
        self._c = ctr

    def wait_ge(self, sem, v):
        return self._e.wait_ge(sem, v)

    def dma_start(self, *a, **k):
        return self._e.dma_start(*a, **k)

    def __getattr__(self, name):
        fn = getattr(self._e, name)
        if name in _CHAIN_OPS:
            def wrap(*a, **k):
                self._e.wait_ge(self._c.sem, self._c.n)
                inst = fn(*a, **k)
                inst.then_inc(self._c.sem, 1)
                self._c.n += 1
                return _CHAINED
            return wrap
        return fn


def build_nc(debug=False, phases=5):
    nc = bass.Bass("TRN2", target_bir_lowering=False)
    es = contextlib.ExitStack()
    nc._es = es

    dm = lambda n, s, t, k="ExternalInput": nc.dram_tensor(n, s, t, kind=k).ap()
    xT = dm("xT", [D, TV], F32)
    flag = dm("flag", [1, 1], F32)
    W_in = dm("W_in", [D, 2 * C2], F32)
    convd = dm("convd", [96, 128, 128], F32)
    conv_b = dm("conv_b", [C2], F32)
    W_xproj = dm("W_xproj", [C2, 2 * NST], F32)
    W_dt = dm("W_dt", [C2, C2], F32)
    b_dt = dm("b_dt", [C2], F32)
    A_log = dm("A_log", [NST], F32)
    D_param = dm("D_param", [C2], F32)
    W_out = dm("W_out", [C2, D], F32)
    centT = dm("centT", [D, NST], F32)
    eb = dm("eb", [NST], F32)
    WrT = dm("WrT", [NEXP, D, D], F32)
    br = dm("br", [NEXP, D], F32)
    WsT = dm("WsT", [2, D, D], F32)
    bs = dm("bs", [2, D], F32)
    id16f = dm("id16f", [16, 16], F32)
    id128f = dm("id128f", [128, 128], F32)

    spk = "ExternalOutput" if debug else "Internal"
    u_sp = dm("u_sp", [NCH, 128, CT * TA], F32, spk)
    xib_sp = dm("xib_sp", [NCH, 128, CT * TA], BF16, spk)
    sres_sp = dm("sres_sp", [NCH, 128, CT * TA], F32, spk)
    d_sp = dm("d_sp", [NCH, 128, CT * TA], F32, spk)
    g_sp = dm("g_sp", [NCH, 128, CT * TA], BF16, spk)
    u2_sp = dm("u2_sp", [NCC, 128, DT * TC], F32, spk)
    bc_hbm = dm("bc_hbm", [32, TV], F32, spk)
    ffnin_sp = dm("ffnin_sp", [NCC, 128, DT * TC], F32, spk)
    gates_hbm = dm("gates_hbm", [16, TH], F32, spk)
    oT = dm("oT", [D, TH], F32, "ExternalOutput")

    sb = lambda n, s, t: es.enter_context(nc.sbuf_tensor(n, s, t))
    BC_buf = sb("BC_buf", [32, TV], F32)
    A_bc = sb("A_bc", [128, NST], F32)
    A_row = sb("A_row", [1, NST], F32)
    flag_col = sb("flag_col", [128, 1], F32)
    bdt_c = sb("bdt_c", [128, CT], F32)
    cvb_c = sb("cvb_c", [128, CT], F32)
    Dp_c = sb("Dp_c", [128, CT], F32)
    bsum_c = sb("bsum_c", [128, DT], F32)
    bs_t = sb("bs_t", [128, DT], F32)
    eb_col = sb("eb_col", [16, 1], F32)
    sid16 = sb("sid16", [16, 16], F32)
    sid128 = sb("sid128", [128, 128], F32)
    ones1 = sb("ones1", [128, 1], F32)
    onesr = sb("onesr", [1, 128], F32)
    onesb = sb("onesb", [128, TA], BF16)
    neginf = sb("neginf", [128, NST], F32)
    carry = sb("carry", [128, NST * CT], F32)
    scent = sb("scent", [128, DT * NST], BF16)
    u_b = sb("u_b", [128, CT * TA], F32)
    sres_b = sb("sres_b", [128, CT * TA], F32)
    d_b = sb("d_b", [128, CT * TA], F32)
    du_b = sb("du_b", [128, CT * TA], F32)

    PP = es.enter_context(nc.psum_tensor("PP", [128, 4096], F32))
    pb = [PP[:, i * 512:(i + 1) * 512] for i in range(8)]

    cL = Ctr(es, nc, "cL")
    cS = Ctr(es, nc, "cS")
    cT = Ctr(es, nc, "cT")
    cA = Ctr(es, nc, "cA")
    cV = Ctr(es, nc, "cV")
    cPh = Ctr(es, nc, "cPh")
    rA = Rot(es, nc, "rA")
    rB = Rot(es, nc, "rB")
    rSl = Rot(es, nc, "rSl")
    rY = Rot(es, nc, "rY")
    rC = Rot(es, nc, "rC")
    rDu = Rot(es, nc, "rDu")
    ROTS = [rA, rB, rSl, rY, rC, rDu]

    blk = es.enter_context(nc.Block())

    def emit(_):
        te = nc.tensor
        ve = Chain(nc.vector, cV)
        se = Chain(nc.scalar, cA)
        ge = DmaQ(nc.gpsimd, cL)
        sy = DmaQ(nc.sync, cS)
        geE = nc.gpsimd
        syE = nc.sync

        pool_state = [None] * 8
        pool_rr = [0]

        def acquire_bank():
            i = pool_rr[0] % 8
            pool_rr[0] += 1
            w = pool_state[i]
            if w is not None:
                te.wait_ge(w[0].sem, w[1])
            pool_state[i] = None
            return i

        def release_bank(i, ctr):
            pool_state[i] = (ctr, ctr.n)

        def acquire2():
            if pool_rr[0] % 2 == 1:
                pool_rr[0] += 1
            i = pool_rr[0] % 8
            pool_rr[0] += 2
            for j in (i, i + 1):
                w = pool_state[j]
                if w is not None:
                    te.wait_ge(w[0].sem, w[1])
                pool_state[j] = None
            return i

        def release2(i, ctr):
            pool_state[i] = (ctr, ctr.n)
            pool_state[i + 1] = (ctr, ctr.n)

        def pool_reset():
            for i in range(8):
                w = pool_state[i]
                if w is not None:
                    te.wait_ge(w[0].sem, w[1])
                pool_state[i] = None
            pool_rr[0] = 0

        def pairview(i, w):
            return PP[:, i * 512:(i + 2) * 512].rearrange(
                "p (b n) -> p b n", n=512)[:, :, 0:w]

        def ph_barrier():
            for eng in (te, ve, se, ge, sy):
                for ctr in (cT, cV, cA, cL, cS):
                    eng.wait_ge(ctr.sem, ctr.n)
                for rot in ROTS:
                    for sem, n in zip(rot.sems, rot.ns):
                        if n:
                            eng.wait_ge(sem, n)

        def emit_ln(in_b, out_b, W, sq_b, stat, mu_bc, rs_bc):
            i3 = in_b.rearrange("p (d t) -> p d t", t=W)
            se.wait_ge(cV.sem, cV.n)
            se.wait_ge(cS.sem, cS.n)
            se.wait_ge(cT.sem, cT.n)
            cA.inc(se.activation(out=sq_b[:, 0:DT * W], in_=in_b, func=AF.Square))
            sq_A = cA.n
            b1 = acquire_bank()
            b2 = acquire_bank()
            te.wait_ge(cA.sem, sq_A)
            for dt_i in range(DT):
                cT.inc(te.matmul(pb[b1][0:1, 0:W], ones1[:], i3[:, dt_i, :],
                                 start=(dt_i == 0), stop=(dt_i == DT - 1)), 1 if dt_i == DT - 1 else 0)
            sx_T = cT.n
            sq3 = sq_b[:, 0:DT * W].rearrange("p (d t) -> p d t", t=W)
            for dt_i in range(DT):
                cT.inc(te.matmul(pb[b2][0:1, 0:W], ones1[:], sq3[:, dt_i, :],
                                 start=(dt_i == 0), stop=(dt_i == DT - 1)), 1 if dt_i == DT - 1 else 0)
            sq_T = cT.n
            ve.wait_ge(cT.sem, sq_T)
            cV.inc(ve.tensor_scalar_mul(stat[0:1, 0 * TC:0 * TC + W], pb[b1][0:1, 0:W], 1.0 / D))
            cV.inc(ve.tensor_scalar_mul(stat[0:1, 1 * TC:1 * TC + W], pb[b2][0:1, 0:W], 1.0 / D))
            release_bank(b1, cV)
            release_bank(b2, cV)
            cV.inc(ve.tensor_mul(stat[0:1, 2 * TC:2 * TC + W], stat[0:1, 0 * TC:0 * TC + W], stat[0:1, 0 * TC:0 * TC + W]))
            cV.inc(ve.tensor_sub(stat[0:1, 1 * TC:1 * TC + W], stat[0:1, 1 * TC:1 * TC + W], stat[0:1, 2 * TC:2 * TC + W]))
            cV.inc(ve.tensor_scalar_add(stat[0:1, 1 * TC:1 * TC + W], stat[0:1, 1 * TC:1 * TC + W], EPS))
            cV.inc(ve.reciprocal(stat[0:1, 2 * TC:2 * TC + W], stat[0:1, 1 * TC:1 * TC + W]))
            rec_V = cV.n
            se.wait_ge(cV.sem, rec_V)
            cA.inc(se.activation(out=stat[0:1, 3 * TC:3 * TC + W], in_=stat[0:1, 2 * TC:2 * TC + W], func=AF.Sqrt))
            ve.wait_ge(cA.sem, cA.n)
            cV.inc(ve.tensor_mul(stat[0:1, 4 * TC:4 * TC + W], stat[0:1, 3 * TC:3 * TC + W], stat[0:1, 3 * TC:3 * TC + W]))
            cV.inc(ve.tensor_mul(stat[0:1, 4 * TC:4 * TC + W], stat[0:1, 4 * TC:4 * TC + W], stat[0:1, 1 * TC:1 * TC + W]))
            cV.inc(ve.tensor_scalar(stat[0:1, 4 * TC:4 * TC + W], stat[0:1, 4 * TC:4 * TC + W], -1.0, 3.0,
                                    ALU.mult, ALU.add))
            cV.inc(ve.tensor_mul(stat[0:1, 4 * TC:4 * TC + W], stat[0:1, 4 * TC:4 * TC + W], stat[0:1, 3 * TC:3 * TC + W]))
            cV.inc(ve.tensor_scalar_mul(stat[0:1, 4 * TC:4 * TC + W], stat[0:1, 4 * TC:4 * TC + W], 0.5))
            st_V = cV.n
            te.wait_ge(cV.sem, st_V)
            cT.inc(te.matmul(pb[b1][:, 0:W], onesr[:], stat[0:1, 0 * TC:0 * TC + W],
                             start=True, stop=True), 1)
            cT.inc(te.matmul(pb[b2][:, 0:W], onesr[:], stat[0:1, 4 * TC:4 * TC + W],
                             start=True, stop=True), 1)
            bc_T = cT.n
            ve.wait_ge(cT.sem, bc_T)
            mu3 = pb[b1][:, 0:W].rearrange("p (o t) -> p o t", o=1).broadcast_to([128, DT, W])
            rs3 = pb[b2][:, 0:W].rearrange("p (o t) -> p o t", o=1).broadcast_to([128, DT, W])
            o3 = out_b.rearrange("p (d t) -> p d t", t=W)
            cV.inc(ve.tensor_tensor(o3, i3, mu3, ALU.subtract))
            cV.inc(ve.tensor_tensor(o3, o3, rs3, ALU.mult))
            release_bank(b1, cV)
            release_bank(b2, cV)
            return cV.n

        esA = contextlib.ExitStack()
        sbA = lambda n, s, t: esA.enter_context(nc.sbuf_tensor(n, s, t))
        wi = sbA("wi", [128, 8 * C2], BF16)
        wres = sbA("wres", [128, 8 * C2], BF16)
        cvd = sbA("cvd", [128, 96 * 128], BF16)
        wxp = sbA("wxp", [128, CT * 32], BF16)
        xib_b = sbA("xib_b", [128, CT * TA], BF16)
        cL.inc(ge.dma_start(out=wi[:].rearrange("p (k c) -> p k c", c=C2),
                            in_=W_in.rearrange("(k p) c -> p k c", p=128)[:, :, 0:C2]), 16)
        cL.inc(ge.dma_start(out=wres[:].rearrange("p (k c) -> p k c", c=C2),
                            in_=W_in.rearrange("(k p) c -> p k c", p=128)[:, :, C2:2 * C2]), 16)
        cL.inc(ge.dma_start(out=cvd[:].rearrange("p (i n) -> p i n", n=128),
                            in_=convd.rearrange("i p n -> p i n")), 16)
        cL.inc(ge.dma_start(out=wxp[:].rearrange("p (k c) -> p k c", c=32),
                            in_=W_xproj.rearrange("(k p) c -> p k c", p=128)), 16)
        wload = cL.n
        cS.inc(sy.dma_start(out=A_row[:], in_=A_log.rearrange("(o n) -> o n", o=1)), 16)
        cS.inc(sy.dma_start(out=flag_col[0:1, :], in_=flag), 16)
        cS.inc(sy.dma_start(out=bdt_c[:],
                            in_=b_dt.rearrange("(c p) -> p c", p=128),
                            allow_slow_non_contiguous=True), 16)
        cS.inc(sy.dma_start(out=cvb_c[:],
                            in_=conv_b.rearrange("(c p) -> p c", p=128),
                            allow_slow_non_contiguous=True), 16)
        cS.inc(sy.dma_start(out=Dp_c[:],
                            in_=D_param.rearrange("(c p) -> p c", p=128),
                            allow_slow_non_contiguous=True), 16)
        cS.inc(sy.dma_start(out=bsum_c[:],
                            in_=bs[0].rearrange("(c p) -> p c", p=128),
                            allow_slow_non_contiguous=True), 16)
        cS.inc(sy.dma_start(out=bs_t[:],
                            in_=bs[1].rearrange("(c p) -> p c", p=128),
                            allow_slow_non_contiguous=True), 16)
        cS.inc(sy.dma_start(out=eb_col[:], in_=eb.rearrange("(n o) -> n o", o=1)), 16)
        cS.inc(sy.dma_start(out=sid16[:], in_=id16f), 16)
        cS.inc(sy.dma_start(out=sid128[:], in_=id128f), 16)
        cL.inc(ge.dma_start(out=scent[:].rearrange("p (k c) -> p k c", c=NST),
                            in_=centT.rearrange("(k p) c -> p k c", p=128)), 16)
        setup_S = cS.n
        se.wait_ge(cS.sem, setup_S)
        cA.inc(se.activation(out=A_row[:], in_=A_row[:], func=AF.Exp))
        ve.wait_ge(cA.sem, cA.n)
        ve.wait_ge(cS.sem, setup_S)
        cV.inc(ve.tensor_scalar_mul(A_row[:], A_row[:], -1.0))
        cV.inc(ve.tensor_add(bsum_c[:], bsum_c[:], bs_t[:]))
        cV.inc(ve.tensor_scalar_mul(bsum_c[:], bsum_c[:], 0.5))
        cV.inc(ve.memset(neginf[:], -3.0e38))
        cV.inc(ve.memset(ones1[:], 1.0))
        cV.inc(ve.memset(onesr[:], 1.0))
        cV.inc(ve.memset(onesb[:], 1.0))
        cV.inc(ve.memset(carry[:], 0.0))
        sy.wait_ge(cV.sem, cV.n)
        cS.inc(sy.dma_start(out=A_bc[0:1, :], in_=A_row[:]), 16)
        for k in [1, 2, 4, 8, 16, 32, 64]:
            sy.wait_ge(cS.sem, cS.n)
            cS.inc(sy.dma_start(out=A_bc[k:2 * k, :], in_=A_bc[0:k, :]), 16)
        for k in [1, 2, 4, 8, 16, 32, 64]:
            sy.wait_ge(cS.sem, cS.n)
            cS.inc(sy.dma_start(out=flag_col[k:2 * k, :], in_=flag_col[0:k, :]), 16)
        consts_S = cS.n

        xcb = [sbA(f"xcb{i}", [128, 8 * TA], BF16) for i in range(2)]
        xiraw = sbA("xiraw", [128, CT * (TA + 3)], BF16)
        xr3 = xiraw[:].rearrange("p (c n) -> p c n", n=TA + 3)
        cV.inc(ve.memset(xr3[:, :, 0:3], 0.0))
        halo_v = cV.n
        xcons = [0, 0]
        spill_tok = None
        conv_T = 0
        se.wait_ge(cS.sem, consts_S)
        ve.wait_ge(cS.sem, consts_S)
        for ch in range(NCH):
            bi = ch % 2
            if xcons[bi]:
                ge.wait_ge(cT.sem, xcons[bi])
            cL.inc(ge.dma_start(
                out=xcb[bi][:].rearrange("p (k t) -> p k t", t=TA),
                in_=xT.rearrange("(k p) t -> p k t", p=128)[:, :, ch * TA:(ch + 1) * TA]), 16)
            te.wait_ge(cL.sem, max(cL.n, wload))
            ve.wait_ge(cV.sem, halo_v)
            if spill_tok:
                ve.wait_ge(*spill_tok)
            if ch:
                ve.wait_ge(cT.sem, conv_T)
            for mp in range(CT // 2):
                b = acquire2()
                for mt in (2 * mp, 2 * mp + 1):
                    bb = b + (mt & 1)
                    for kt in range(8):
                        cT.inc(te.matmul(pb[bb][:, 0:TA],
                                         wi[:, kt * C2 + mt * 128: kt * C2 + (mt + 1) * 128],
                                         xcb[bi][:, kt * TA:(kt + 1) * TA],
                                         start=(kt == 0), stop=(kt == 7)), 1 if kt == 7 else 0)
                ve.wait_ge(cT.sem, cT.n)
                cV.inc(ve.tensor_copy(xr3[:, 2 * mp:2 * mp + 2, 3:3 + TA],
                                      pairview(b, TA)))
                release2(b, cV)
            xi_raw_A = cV.n
            te.wait_ge(cV.sem, xi_raw_A)
            for cp in range(CT // 2):
                b = acquire2()
                for ct in (2 * cp, 2 * cp + 1):
                    bb = b + (ct & 1)
                    for tap in range(4):
                        cT.inc(te.matmul(pb[bb][:, 0:TA],
                                         cvd[:, (tap * 16 + ct) * 128:(tap * 16 + ct + 1) * 128],
                                         xr3[:, ct, tap:tap + TA],
                                         start=(tap == 0), stop=False), 0)
                    cT.inc(te.matmul(pb[bb][:, 0:TA],
                                     cvd[:, (64 + ct) * 128:(64 + ct + 1) * 128],
                                     onesb[:, 0:TA], start=False, stop=True), 1)
                se.wait_ge(cT.sem, cT.n)
                cA.inc(se.activation(
                    out=u_b[:, 2 * cp * TA:(2 * cp + 2) * TA].rearrange(
                        "p (b n) -> p b n", n=TA),
                    in_=pairview(b, TA), func=AF.Silu))
                release2(b, cA)
            conv_T = cT.n
            u_A = cA.n
            ve.wait_ge(cT.sem, conv_T)
            cV.inc(ve.tensor_copy(xr3[:, :, 0:3], xr3[:, :, TA:TA + 3]))
            halo_v = cV.n
            ve.wait_ge(cA.sem, u_A)
            if spill_tok:
                ve.wait_ge(*spill_tok)
            cV.inc(ve.tensor_copy(xib_b[:], u_b[:]))
            xib_V = cV.n
            b = acquire_bank()
            te.wait_ge(cV.sem, xib_V)
            for kt in range(CT):
                cT.inc(te.matmul(pb[b][0:32, 0:TA], wxp[:, kt * 32:(kt + 1) * 32],
                                 xib_b[:, kt * TA:(kt + 1) * TA],
                                 start=(kt == 0), stop=(kt == CT - 1)), 1 if kt == CT - 1 else 0)
            se.wait_ge(cT.sem, cT.n)
            cA.inc(se.activation(out=BC_buf[:, ch * TA:(ch + 1) * TA], in_=pb[b][0:32, 0:TA],
                                 func=AF.Copy))
            release_bank(b, cA)
            sres_A = 0
            if ch >= NRE:
                for mp in range(CT // 2):
                    b = acquire2()
                    for mt in (2 * mp, 2 * mp + 1):
                        bb = b + (mt & 1)
                        for kt in range(8):
                            cT.inc(te.matmul(pb[bb][:, 0:TA],
                                             wres[:, kt * C2 + mt * 128: kt * C2 + (mt + 1) * 128],
                                             xcb[bi][:, kt * TA:(kt + 1) * TA],
                                             start=(kt == 0), stop=(kt == 7)), 1 if kt == 7 else 0)
                    se.wait_ge(cT.sem, cT.n)
                    cA.inc(se.activation(
                        out=sres_b[:, 2 * mp * TA:(2 * mp + 2) * TA].rearrange(
                            "p (b n) -> p b n", n=TA),
                        in_=pairview(b, TA), func=AF.Silu))
                    release2(b, cA)
                sres_A = cA.n
            xcons[bi] = cT.n
            rA.begin()
            syE.wait_ge(cA.sem, u_A)
            rA.dma(syE, out=u_sp[ch], in_=u_b[:])
            syE.wait_ge(cV.sem, xib_V)
            rA.dma(syE, out=xib_sp[ch], in_=xib_b[:])
            if ch >= NRE:
                syE.wait_ge(cA.sem, sres_A)
                rA.dma(syE, out=sres_sp[ch], in_=sres_b[:])
            spill_tok = rA.tok()

        sy.wait_ge(cA.sem, cA.n)
        cS.inc(sy.dma_start(out=bc_hbm, in_=BC_buf[:]), 16)
        ph_barrier()
        esA.close()
        if phases < 2:
            return

        esB = contextlib.ExitStack()
        sbB = lambda n, s, t: esB.enter_context(nc.sbuf_tensor(n, s, t))
        TAp = TA + 1
        dA_b = [sbB(f"dAb{i}", [128, CT * TAp], F32) for i in range(2)]
        dBu_b = sbB("dBu", [128, CT * TAp], F32)
        h_b = sbB("h_b", [128, CT * TAp], F32)
        y_acc = sbB("y_acc", [128, CT * TA], F32)
        Bb_b = sbB("Bb_b", [128, NST * TA], F32)
        Cb_b = sbB("Cb_b", [128, NST * TA], F32)
        Bbc = Bb_b[:]
        Cbc = Cb_b[:]

        def bchain(eng, buf, r0, win):
            rB.dma(eng,
                   out=buf[:].rearrange("p (n t) -> p n t", n=NST),
                   in_=bc_hbm[r0:r0 + NST, win * TA:(win + 1) * TA][None]
                   .broadcast_to([128, NST, TA]))
        g_b = sbB("g_b", [128, CT * TA], BF16)
        dA3 = [dA_b[i][:].rearrange("p (c n) -> p c n", n=TAp) for i in range(2)]
        dBu3 = dBu_b[:].rearrange("p (c n) -> p c n", n=TAp)
        h3 = h_b[:].rearrange("p (c n) -> p c n", n=TAp)
        d3 = d_b[:].rearrange("p (c t) -> p c t", t=TA)
        du3 = du_b[:].rearrange("p (c t) -> p c t", t=TA)
        y3 = y_acc[:].rearrange("p (c t) -> p c t", t=TA)
        ve.wait_ge(cS.sem, consts_S)
        for i in range(2):
            cV.inc(ve.memset(dA3[i][:, :, 0:1], 0.0))
        pad_V = cV.n
        pool_reset()
        te.wait_ge(cS.sem, consts_S)

        wdt_sl = Cb_b[:].bitcast(BF16)
        xcv = y_acc[:].bitcast(BF16)
        xib2 = xcv[:, 0:CT * TA]
        e1p = g_b[:].bitcast(F32)
        d2_b = sres_b
        cL.inc(ge.dma_start(
            out=xcv[:, CT * TA:CT * TA + 2048].rearrange("p (i n) -> p i n", n=128),
            in_=convd.rearrange("i p n -> p i n")[:, 80:96, :]), 16)
        cvd2_L = cL.n
        WdtR = W_dt.rearrange("(k p) c -> p k c", p=128)
        a2st = dict(mmend=0, slfree=[0, 0], spar=0, spill=0, lastA=0)
        a2_spill = [0] * NCH
        a2evacs = []

        def a2_group(k, mp):
            if mp == 0:
                sy.wait_ge(cT.sem, a2st['mmend'])
                cS.inc(sy.dma_start(out=xib2, in_=xib_sp[k]), 16)
                a2st['xload'] = cS.n
            sl = a2st['spar']
            a2st['spar'] ^= 1
            slab = wdt_sl[:, sl * 4096:(sl + 1) * 4096]
            rSl.begin()
            geE.wait_ge(cT.sem, a2st['slfree'][sl])
            rSl.dma(geE,
                    out=slab.rearrange("p (j c) -> p j c", c=256),
                    in_=WdtR[:, :, 2 * mp * 128:(2 * mp + 2) * 128])
            te.wait_ge(*rSl.tok())
            if mp == 0:
                te.wait_ge(cS.sem, a2st['xload'])
                te.wait_ge(cL.sem, cvd2_L)
            b = acquire2()
            for i in (0, 1):
                mt = 2 * mp + i
                bb = b + i
                for kt in range(CT):
                    cT.inc(te.matmul(pb[bb][:, 0:TA],
                                     slab[:, kt * 256 + i * 128: kt * 256 + (i + 1) * 128],
                                     xib2[:, kt * TA:(kt + 1) * TA],
                                     start=(kt == 0), stop=False), 0)
                cT.inc(te.matmul(pb[bb][:, 0:TA],
                                 xcv[:, CT * TA + mt * 128:CT * TA + (mt + 1) * 128],
                                 onesb[:, 0:TA], start=False, stop=True), 1)
            a2st['slfree'][sl] = cT.n
            if mp == CT // 2 - 1:
                a2st['mmend'] = cT.n
            a2evacs.append((b, k, mp, cT.n))

        def a2_evac(b, k, mp, mmT):
            se.wait_ge(cT.sem, mmT)
            cA.inc(se.activation(
                out=e1p[:, 0:2 * TA].rearrange("p (b n) -> p b n", n=TA),
                in_=pairview(b, TA), func=AF.Exp))
            if mp == 0 and a2st['spill']:
                se.wait_ge(cL.sem, a2st['spill'])
            cA.inc(se.activation(
                out=d2_b[:, 2 * mp * TA:(2 * mp + 2) * TA].rearrange(
                    "p (b n) -> p b n", n=TA),
                in_=e1p[:, 0:2 * TA].rearrange("p (b n) -> p b n", n=TA),
                func=AF.Ln, bias=1.0))
            release2(b, cA)
            a2st['lastA'] = cA.n
            if mp == CT // 2 - 1:
                ge.wait_ge(cA.sem, cA.n)
                cL.inc(ge.dma_start(out=d_sp[k], in_=d2_b[:]), 16)
                a2st['spill'] = cL.n
                a2_spill[k] = cL.n

        for mp in range(CT // 2):
            if len(a2evacs) >= 4:
                a2_evac(*a2evacs.pop(0))
            a2_group(0, mp)
        while a2evacs:
            a2_evac(*a2evacs.pop(0))
        rB.begin()
        syE.wait_ge(cL.sem, cL.n)
        rB.dma(syE, out=d_b[:], in_=d_sp[0])
        bchain(syE, Bb_b, 0, 0)
        ev_next = rB.tok()
        rDu.begin()
        geE.wait_ge(cL.sem, cL.n)
        rDu.dma(geE, out=du_b[:], in_=u_sp[0])
        du_next = rDu.tok()
        for mp in range(CT // 2):
            if len(a2evacs) >= 4:
                a2_evac(*a2evacs.pop(0))
            a2_group(1, mp)
        while a2evacs:
            a2_evac(*a2evacs.pop(0))

        zmm_T = 0
        prev_scan = 0
        dfree_A = 0
        dufree_V = 0
        dumult_V = 0
        gspill_L = 0
        yld_next = None
        for ch in range(NCH):
            a2_done = 0
            real = ch >= NRE
            ev_cur = ev_next
            if ch <= NRE - 2:
                a2_sched = [(2 * ch + 2, mp) for mp in range(CT // 2)] + \
                           [(2 * ch + 3, mp) for mp in range(CT // 2)]
                a2_target = [0, 0, 0, 0, 1, 2, 3, 4, 5, 6, 7, 8, 10, 12, 14, 15]
            else:
                a2_sched = None
            if ch == NRE:
                zmm_T = a2st['mmend']
                ve.wait_ge(cS.sem, consts_S)
                ve.wait_ge(cA.sem, cA.n)
                cV.inc(ve.tensor_scalar_mul(carry[:], carry[:], flag_col[:]))
                prev_scan = cV.n
            du_cur = du_next
            if real:
                cb_cur = cb_next
            ve.wait_ge(*ev_cur)
            ve.wait_ge(*du_cur)
            cV.inc(ve.tensor_mul(du_b[:], d_b[:], du_b[:]))
            dumult_V = cV.n
            se.wait_ge(*ev_cur)
            se.wait_ge(cV.sem, pad_V)
            if ch + 1 < NCH:
                rB.begin()
            dA_used = [0, 0]

            def emit_dA(n):
                pi = n % 2
                if dA_used[pi]:
                    se.wait_ge(cV.sem, dA_used[pi])
                cA.inc(se.activation(out=dA3[pi][:, :, 1:TAp], in_=d3,
                                     func=AF.Exp, scale=A_bc[:, n:n + 1]))
                return cA.n

            def emit_pad(n):
                se.wait_ge(cV.sem, prev_scan)
                cA.inc(se.activation(
                    out=dBu3[:, :, 0:1],
                    in_=carry[:, n * CT:(n + 1) * CT].rearrange("p (c o) -> p c o", o=1),
                    func=AF.Copy))
                return cA.n

            emit_dA(0)
            actn = emit_pad(0)
            for n in range(NST):
                if a2_sched is not None:
                    while a2_done < a2_target[n]:
                        a2_evac(*a2evacs.pop(0))
                        a2_done += 1
                    a2_group(*a2_sched[n])
                pi = n % 2
                bview = Bbc[:, n * TA:(n + 1) * TA].rearrange(
                    "p (o t) -> p o t", o=1).broadcast_to([128, CT, TA])
                cV.inc(ve.tensor_tensor(dBu3[:, :, 1:TAp], du3, bview, ALU.mult))
                if n == NST - 1:
                    dufree_V = cV.n
                    if ch + 1 < NCH:
                        syE.wait_ge(cL.sem, a2_spill[ch + 1])
                        syE.wait_ge(cA.sem, dfree_A)
                        syE.wait_ge(cV.sem, dumult_V)
                        rB.dma(syE, out=d_b[:], in_=d_sp[ch + 1])
                        syE.wait_ge(cV.sem, dufree_V)
                        bchain(syE, Bb_b, 0, ch + 1)
                        rDu.begin()
                        geE.wait_ge(cV.sem, dufree_V)
                        rDu.dma(geE, out=du_b[:], in_=u_sp[ch + 1])
                        du_next = rDu.tok()
                ve.wait_ge(cA.sem, actn)
                cV.inc(ve.tensor_tensor_scan(out=h_b[:], data0=dA_b[pi][:],
                                             data1=dBu_b[:], initial=0.0,
                                             op0=ALU.mult, op1=ALU.add))
                scan_V = cV.n
                prev_scan = scan_V
                dA_used[pi] = cV.n
                if n + 1 < NST:
                    dn = emit_dA(n + 1)
                    if n + 1 == NST - 1:
                        dfree_A = dn
                se.wait_ge(cV.sem, scan_V)
                cA.inc(se.activation(
                    out=carry[:, n * CT:(n + 1) * CT].rearrange("p (c o) -> p c o", o=1),
                    in_=h3[:, :, TA:TAp], func=AF.Copy))
                if n + 1 < NST:
                    actn = emit_pad(n + 1)
                if real:
                    cview = Cbc[:, n * TA:(n + 1) * TA].rearrange(
                        "p (o t) -> p o t", o=1).broadcast_to([128, CT, TA])
                    if zmm_T:
                        ve.wait_ge(cT.sem, zmm_T)
                    if n == 0:
                        ve.wait_ge(*cb_cur)
                    cV.inc(ve.tensor_tensor(y3, h3[:, :, 1:TAp], cview, ALU.mult))
                    if ch == NRE and n == 0:
                        te.wait_ge(cA.sem, a2st['lastA'])
                    te.wait_ge(cV.sem, cV.n)
                    for zb in range(8):
                        cT.inc(te.matmul(
                            PP[:, zb * 512:(zb + 1) * 512],
                            sid128[:], y_acc[:, zb * 512:(zb + 1) * 512],
                            start=(n == 0), stop=(n == NST - 1)),
                               1 if zb == 7 else 0)
                    zmm_T = cT.n
            while a2evacs:
                a2_evac(*a2evacs.pop(0))
            if ch + 1 < NCH:
                if ch + 1 == NRE:
                    rY.begin()
                    syE.wait_ge(cL.sem, a2st['spill'])
                    rY.dma(syE, out=sres_b[:], in_=sres_sp[NRE])
                    rY.dma(syE, out=u_b[:], in_=u_sp[NRE])
                    yld_next = rY.tok()
                if ch + 1 >= NRE:
                    rC.begin()
                    if ch + 1 == NRE:
                        geE.wait_ge(cT.sem, a2st['mmend'])
                    geE.wait_ge(cV.sem, cV.n if real else dufree_V)
                    rC.dma(geE,
                           out=Cb_b[:].rearrange("p (n t) -> p n t", n=NST),
                           in_=bc_hbm[NST:2 * NST,
                                      (ch + 1) * TA:(ch + 2) * TA][None]
                           .broadcast_to([128, NST, TA]))
                    cb_next = rC.tok()
            if real:
                yld_cur = yld_next
                ve.wait_ge(cT.sem, zmm_T)
                ve.wait_ge(*yld_cur)
                for ct in range(CT):
                    cV.inc(ve.scalar_tensor_tensor(
                        out=y_acc[:, ct * TA:(ct + 1) * TA],
                        in0=u_b[:, ct * TA:(ct + 1) * TA],
                        scalar=Dp_c[:, ct:ct + 1],
                        in1=PP[:, ct * TA:(ct + 1) * TA],
                        op0=ALU.mult, op1=ALU.add))
                if ch == NRE:
                    ve.wait_ge(cA.sem, a2st['lastA'])
                ve.wait_ge(cL.sem, gspill_L)
                cV.inc(ve.tensor_mul(g_b[:], y_acc[:], sres_b[:]))
                ge.wait_ge(cV.sem, cV.n)
                cL.inc(ge.dma_start(out=g_sp[ch], in_=g_b[:]), 16)
                gspill_L = cL.n
                if ch + 1 < NCH:
                    rY.begin()
                    syE.wait_ge(cV.sem, cV.n)
                    rY.dma(syE, out=u_b[:], in_=u_sp[ch + 1])
                    rY.dma(syE, out=sres_b[:], in_=sres_sp[ch + 1])
                    yld_next = rY.tok()
            if ch + 1 < NCH:
                ev_next = rB.tok()

        ph_barrier()
        esB.close()
        if phases < 4:
            return

        u2_b = sb("u2_b", [128, DT * TC], F32)
        u2b_b = sb("u2b_b", [128, DT * TC], BF16)
        gates_r = sb("gates_r", [16, TH], F32)
        gates_rb = sb("gates_rb", [16, TH], BF16)
        stat = sb("stat", [1, 5 * TC], F32)
        mu_bc = sb("mu_bc", [128, TC], F32)
        rs_bc = sb("rs_bc", [128, TC], F32)
        esC = contextlib.ExitStack()
        sbC = lambda n, s, t: esC.enter_context(nc.sbuf_tensor(n, s, t))
        wout = sbC("wout", [128, CT * D], BF16)
        gbig = sbC("gbig", [128, CT * TC], BF16)
        resid2 = sbC("resid2", [128, DT * TC], F32)
        sc_b = sbC("sc_b", [16, TC], F32)
        sT_b = sbC("sT_b", [128, 4 * NST], F32)
        m_b = sbC("m_b", [128, 8], F32)
        msk_b = sbC("msk_b", [128, NST], U8)
        gT_b = sbC("gT_b", [128, 4 * NST], F32)
        cL.inc(ge.dma_start(out=wout[:].rearrange("p (k c) -> p k c", c=D),
                            in_=W_out.rearrange("(k p) c -> p k c", p=128)), 16)
        wout_L = cL.n
        te.wait_ge(cS.sem, consts_S)
        ve.wait_ge(cS.sem, consts_S)
        xc_par = [d_b, du_b]
        gb_par = [gbig[:], sres_b[:].bitcast(BF16)[:, 0:CT * TC]]
        rs_par = [u_b, resid2]
        ldc_tok = [None, None]
        mmend_par = [0, 0]
        evacV_par = [0, 0]

        def c_loads(cc):
            par = cc % 2
            rB.begin()
            syE.wait_ge(cT.sem, mmend_par[par])
            gv3 = gb_par[par].rearrange("p (c t) -> p c t", t=TC)
            rB.dma(syE, out=gv3[:, :, 0:TA],
                   in_=g_sp[NRE + 2 * cc].rearrange("p (c t) -> p c t", t=TA))
            rB.dma(syE, out=gv3[:, :, TA:TC],
                   in_=g_sp[NRE + 2 * cc + 1].rearrange("p (c t) -> p c t", t=TA))
            syE.wait_ge(cV.sem, evacV_par[par])
            rB.dma(syE, out=xc_par[par][:].rearrange("p (k t) -> p k t", t=TC),
                   in_=xT.rearrange("(k p) t -> p k t", p=128)[
                       :, :, TH + cc * TC:TH + (cc + 1) * TC])
            ldc_tok[par] = rB.tok()

        c_loads(0)
        for cc in range(NCC):
            par = cc % 2
            if cc + 1 < NCC:
                c_loads(cc + 1)
            te.wait_ge(*ldc_tok[par])
            te.wait_ge(cL.sem, wout_L)
            ve.wait_ge(*ldc_tok[par])
            for mt in range(DT):
                b = acquire_bank()
                for kt in range(CT):
                    cT.inc(te.matmul(pb[b][:, 0:TC],
                                     wout[:, kt * D + mt * 128: kt * D + (mt + 1) * 128],
                                     gb_par[par][:, kt * TC:(kt + 1) * TC],
                                     start=(kt == 0), stop=(kt == CT - 1)), 1 if kt == CT - 1 else 0)
                ve.wait_ge(cT.sem, cT.n)
                cV.inc(ve.tensor_add(rs_par[par][:, mt * TC:(mt + 1) * TC], pb[b][:, 0:TC],
                                     xc_par[par][:, mt * TC:(mt + 1) * TC]))
                release_bank(b, cV)
            mmend_par[par] = cT.n
            evacV_par[par] = cV.n
            emit_ln(rs_par[par][:], u2_b[:], TC, u2_b[:], stat, mu_bc, rs_bc)
            ve.wait_ge(cT.sem, cT.n)
            cV.inc(ve.tensor_copy(u2b_b[:], u2_b[:]))
            u2_V = cV.n
            sy.wait_ge(cV.sem, u2_V)
            cS.inc(sy.dma_start(out=u2_sp[cc], in_=u2_b[:]), 16)
            b = acquire_bank()
            te.wait_ge(cV.sem, u2_V)
            for kt in range(DT):
                cT.inc(te.matmul(pb[b][0:16, 0:TC], scent[:, kt * NST:(kt + 1) * NST],
                                 u2b_b[:].rearrange("p (k t) -> p k t", t=TC)[:, kt, :],
                                 start=(kt == 0), stop=(kt == DT - 1)), 1 if kt == DT - 1 else 0)
            sc_T = cT.n
            ve.wait_ge(cT.sem, sc_T)
            cV.inc(ve.tensor_scalar(sc_b[0:16, 0:TC], pb[b][0:16, 0:TC],
                                    eb_col[:], None, ALU.add))
            release_bank(b, cV)
            raw_A = cV.n
            bT = acquire_bank()
            te.wait_ge(cV.sem, raw_A)
            for tb in range(4):
                cT.inc(te.transpose(pb[bT][:, tb * NST:(tb + 1) * NST],
                                    sc_b[0:16, tb * 128:(tb + 1) * 128], sid16[:]),
                       1 if tb == 3 else 0)
            ve.wait_ge(cT.sem, cT.n)
            cV.inc(ve.tensor_copy(sT_b[:], pb[bT][:, 0:4 * NST]))
            release_bank(bT, cV)
            sT3 = sT_b[:].rearrange("p (b n) -> p b n", n=NST)
            for tb in range(4):
                cV.inc(ve.reduce_max(m_b[:, tb:tb + 1], sT3[:, tb, :], axis=AX.X))
                cV.inc(ve.tensor_scalar(sT3[:, tb, :], sT3[:, tb, :],
                                        m_b[:, tb:tb + 1], None, ALU.subtract))
            sm_V = cV.n
            se.wait_ge(cV.sem, sm_V)
            cA.inc(se.activation(out=sT_b[:], in_=sT_b[:], func=AF.Exp))
            exp_A = cA.n
            ve.wait_ge(cA.sem, exp_A)
            for tb in range(4):
                cV.inc(ve.reduce_sum(m_b[:, 4 + tb:5 + tb], sT3[:, tb, :], axis=AX.X))
                cV.inc(ve.reciprocal(m_b[:, 4 + tb:5 + tb], m_b[:, 4 + tb:5 + tb]))
                cV.inc(ve.tensor_scalar(sT3[:, tb, :], sT3[:, tb, :],
                                        m_b[:, 4 + tb:5 + tb], None, ALU.mult))
                cV.inc(ve.reduce_max(m_b[:, tb:tb + 1], sT3[:, tb, :], axis=AX.X))
                cV.inc(ve.tensor_scalar(msk_b[:], sT3[:, tb, :], m_b[:, tb:tb + 1],
                                        None, ALU.is_ge))
                cV.inc(ve.tensor_copy(gT_b[:, tb * NST:(tb + 1) * NST], sT3[:, tb, :]))
                cV.inc(ve.copy_predicated(gT_b[:, tb * NST:(tb + 1) * NST], msk_b[:],
                                          neginf[:]))
                cV.inc(ve.reduce_max(m_b[:, tb:tb + 1],
                                     gT_b[:].rearrange("p (b n) -> p b n", n=NST)[:, tb, :],
                                     axis=AX.X))
                cV.inc(ve.tensor_scalar(gT_b[:, tb * NST:(tb + 1) * NST], sT3[:, tb, :],
                                        m_b[:, tb:tb + 1], None, ALU.is_ge))
                cV.inc(ve.tensor_mul(gT_b[:, tb * NST:(tb + 1) * NST],
                                     gT_b[:, tb * NST:(tb + 1) * NST], sT3[:, tb, :]))
            g2_V = cV.n
            bG = acquire_bank()
            te.wait_ge(cV.sem, g2_V)
            for tb in range(4):
                cT.inc(te.transpose(pb[bG][0:16, tb * 128:(tb + 1) * 128],
                                    gT_b[:, tb * NST:(tb + 1) * NST], sid128[:]),
                       1 if tb == 3 else 0)
            se.wait_ge(cT.sem, cT.n)
            cA.inc(se.activation(out=gates_r[:, cc * TC:(cc + 1) * TC],
                                 in_=pb[bG][0:16, 0:TC], func=AF.Copy))
            release_bank(bG, cA)

        cA.inc(se.activation(out=gates_rb[:], in_=gates_r[:], func=AF.Copy))
        sy.wait_ge(cA.sem, cA.n)
        cS.inc(sy.dma_start(out=gates_hbm, in_=gates_r[:]), 16)
        ph_barrier()
        if phases < 5:
            esC.close()
            return

        esC.close()
        esD = contextlib.ExitStack()
        sbD = lambda n, s, t: esD.enter_context(nc.sbuf_tensor(n, s, t))
        wr_b = [sbD(f"wr{i}", [128, DT * D], BF16) for i in range(2)]
        ue_b = [sbD(f"ue{i}", [128, DT * TC], BF16) for i in range(2)]
        gbc2_b = [sbD("gbc_b", [128, 4 * TC], F32),
                  u2b_b[:].bitcast(F32)[:, 0:4 * TC]]
        uh_b = sbD("uh_b", [128, DT * TC], BF16)
        ln2_b = u_b
        out_b = sres_b
        wbr = sbD("wbr", [16, DT * 128], BF16)
        cL.inc(ge.dma_start(out=wbr[:], in_=br), 16)
        wd_L = cL.n
        wr_loaded = [0, 0]
        wr_freeT = [0, 0]
        ue_V = [0, 0]
        ue_freeT = [0, 0]
        gbc_S = [0, 0]
        gbc_cons = [0, 0]

        def gbc_chain(tch, grp):
            par = grp % 2
            buf = gbc2_b[par]
            if gbc_cons[par]:
                sy.wait_ge(cV.sem, gbc_cons[par])
            cS.inc(sy.dma_start(
                out=buf[:].rearrange("p (n t) -> p n t", n=4),
                in_=gates_hbm[4 * grp:4 * grp + 4,
                              tch * TC:(tch + 1) * TC][None]
                .broadcast_to([128, 4, TC])), 16)
            gbc_S[par] = cS.n

        for tch in range(NCC):
            cS.inc(sy.dma_start(out=u2_b[:], in_=u2_sp[tch]), 16)
            u2_S = cS.n
            ve.wait_ge(cS.sem, u2_S)
            ve.wait_ge(cT.sem, cT.n)
            cV.inc(ve.tensor_scalar_mul(uh_b[:], u2_b[:], 0.5))
            base_V = cV.n
            experts = list(range(NEXP)) + [NEXP, NEXP + 1]

            def load_wr(e, wi_):
                if wr_freeT[wi_]:
                    ge.wait_ge(cT.sem, wr_freeT[wi_])
                src_ap = (WrT[e] if e < NEXP else WsT[e - NEXP]).rearrange(
                    "(k p) c -> p k c", p=128)
                cL.inc(ge.dma_start(
                    out=wr_b[wi_][:].rearrange("p (k c) -> p k c", c=D),
                    in_=src_ap), 16)
                wr_loaded[wi_] = cL.n

            load_wr(experts[0], 0)
            load_wr(experts[1], 1)
            gbc_chain(tch, 0)
            gbc_chain(tch, 1)
            banks_d = [acquire_bank() for _ in range(DT)]
            te.wait_ge(cS.sem, u2_S)
            te.wait_ge(cL.sem, wd_L)
            for dt_i in range(DT):
                cT.inc(te.matmul(pb[banks_d[dt_i]][:, 0:TC],
                                 wbr[:, dt_i * 128:(dt_i + 1) * 128],
                                 gates_rb[:, tch * TC:(tch + 1) * TC],
                                 start=True, stop=False),
                       1 if dt_i == DT - 1 else 0)
            for ei, e in enumerate(experts):
                wi_ = ei % 2
                if e < NEXP:
                    par = (e // 4) % 2
                    gv = gbc2_b[par][:, (e % 4) * TC:(e % 4 + 1) * TC].rearrange(
                        "p (o t) -> p o t", o=1).broadcast_to([128, DT, TC])
                    ve.wait_ge(cV.sem, base_V)
                    ve.wait_ge(cS.sem, gbc_S[par])
                    if ue_freeT[wi_]:
                        ve.wait_ge(cT.sem, ue_freeT[wi_])
                    cV.inc(ve.scalar_tensor_tensor(
                        out=ue_b[wi_][:].rearrange("p (k t) -> p k t", t=TC),
                        in0=u2_b[:].rearrange("p (k t) -> p k t", t=TC),
                        scalar=ones1[:], in1=gv, op0=ALU.mult, op1=ALU.mult))
                    ue_V[wi_] = cV.n
                    if e % 4 == 3:
                        gbc_cons[par] = cV.n
                        if e + 8 < NEXP:
                            gbc_chain(tch, e // 4 + 2)
                    mov = ue_b[wi_]
                    te.wait_ge(cV.sem, ue_V[wi_])
                else:
                    mov = uh_b
                    te.wait_ge(cV.sem, base_V)
                te.wait_ge(cL.sem, wr_loaded[wi_])
                last = (ei == len(experts) - 1)
                for dt_i in range(DT):
                    for kt in range(DT):
                        cT.inc(te.matmul(
                            pb[banks_d[dt_i]][:, 0:TC],
                            wr_b[wi_][:, kt * D + dt_i * 128: kt * D + (dt_i + 1) * 128],
                            mov[:, kt * TC:(kt + 1) * TC],
                            start=False, stop=(last and kt == DT - 1)),
                           1 if (dt_i == DT - 1 and kt == DT - 1) else 0)
                ue_freeT[wi_] = cT.n
                wr_freeT[wi_] = cT.n
                if ei + 2 < len(experts):
                    load_wr(experts[ei + 2], (ei + 2) % 2)
            moe_T = cT.n
            ve.wait_ge(cT.sem, moe_T)
            for dt_i in range(DT):
                cV.inc(ve.scalar_tensor_tensor(
                    out=ln2_b[:, dt_i * TC:(dt_i + 1) * TC],
                    in0=pb[banks_d[dt_i]][:, 0:TC], scalar=bsum_c[:, dt_i:dt_i + 1],
                    in1=u2_b[:, dt_i * TC:(dt_i + 1) * TC], op0=ALU.add, op1=ALU.add))
                release_bank(banks_d[dt_i], cV)
            cV.inc(ve.tensor_add(ln2_b[:], ln2_b[:], u2_b[:]))
            sy.wait_ge(cV.sem, cV.n)
            cS.inc(sy.dma_start(out=ffnin_sp[tch], in_=ln2_b[:]), 16)
            emit_ln(ln2_b[:], out_b[:], TC, out_b[:], stat, mu_bc, rs_bc)
            out_V = cV.n
            sy.wait_ge(cV.sem, out_V)
            cS.inc(sy.dma_start(
                out=oT.rearrange("(k p) t -> p k t", p=128)[:, :, tch * TC:(tch + 1) * TC],
                in_=out_b[:].rearrange("p (k t) -> p k t", t=TC)), 16)
            sy.wait_ge(cV.sem, cV.n)

        sy.wait_ge(cS.sem, cS.n)
        ph_barrier()
        esD.close()

    blk.sync(emit)
    es.close()
    return nc


def kernel(L0_attn_x, L0_attn_W_in, L0_attn_conv_w, L0_attn_conv_b,
           L0_attn_W_xproj, L0_attn_W_dt, L0_attn_b_dt, L0_attn_A_log,
           L0_attn_D_param, L0_attn_W_out, L0_ffn_centroids, L0_ffn_expert_biases,
           L0_ffn_Wr, L0_ffn_br, L0_ffn_Ws, L0_ffn_bs, _debug=False, _phases=5):
    x = np.ascontiguousarray(np.asarray(L0_attn_x))
    B = x.shape[0]
    conv_w = np.asarray(L0_attn_conv_w).reshape(C2, 4)
    convd = np.zeros((96, 128, 128), np.float32)
    idx = np.arange(128)
    for tap in range(4):
        for ct in range(CT):
            convd[tap * 16 + ct, idx, idx] = conv_w[ct * 128:(ct + 1) * 128, tap]
    cb = np.asarray(L0_attn_conv_b).reshape(CT, 128)
    bd = np.asarray(L0_attn_b_dt).reshape(CT, 128)
    for ct in range(CT):
        convd[64 + ct, idx, idx] = cb[ct]
        convd[80 + ct, idx, idx] = bd[ct]
    WrT = np.ascontiguousarray(np.transpose(np.asarray(L0_ffn_Wr), (0, 2, 1)))
    WsT = np.ascontiguousarray(np.transpose(np.asarray(L0_ffn_Ws), (0, 2, 1)))
    centT = np.ascontiguousarray(np.asarray(L0_ffn_centroids).T)
    id16f = np.eye(16, dtype=np.float32)

    common = dict(
        W_in=np.ascontiguousarray(L0_attn_W_in), convd=convd,
        conv_b=np.ascontiguousarray(L0_attn_conv_b),
        W_xproj=np.ascontiguousarray(L0_attn_W_xproj),
        W_dt=np.ascontiguousarray(L0_attn_W_dt),
        b_dt=np.ascontiguousarray(L0_attn_b_dt),
        A_log=np.ascontiguousarray(L0_attn_A_log),
        D_param=np.ascontiguousarray(L0_attn_D_param),
        W_out=np.ascontiguousarray(L0_attn_W_out),
        centT=centT, eb=np.ascontiguousarray(L0_ffn_expert_biases),
        WrT=WrT, br=np.ascontiguousarray(L0_ffn_br), WsT=WsT,
        bs=np.ascontiguousarray(L0_ffn_bs), id16f=id16f, id128f=np.eye(128, dtype=np.float32))

    in_maps = []
    for c in range(8):
        b, j = c // 2, c % 2
        if j == 0:
            xv = np.concatenate([np.zeros((TH, D), np.float32), x[b, 0:TH]], 0)
            fl = np.zeros((1, 1), np.float32)
        else:
            xv = np.concatenate([x[b, 0:TH], x[b, TH:]], 0)
            fl = np.ones((1, 1), np.float32)
        m = dict(common)
        m["xT"] = np.ascontiguousarray(xv.T)
        m["flag"] = fl
        in_maps.append(m)

    nc = build_nc(debug=_debug, phases=_phases)
    res = run_bass_kernel_spmd(nc, in_maps, core_ids=list(range(8)))
    out = np.empty((B, TV, D), np.float32)
    for c in range(8):
        b, j = c // 2, c % 2
        out[b, j * TH:(j + 1) * TH, :] = res.results[c]["oT"].T
    if _debug:
        return out, res
    return out


# version 20
# speedup vs baseline: 1.0358x; 1.0286x over previous
import contextlib
import os
import numpy as np
import ml_dtypes
import concourse.bass as bass
import concourse.mybir as mybir
from concourse.bass_utils import run_bass_kernel_spmd

F32 = mybir.dt.float32
BF16 = mybir.dt.bfloat16
U8 = mybir.dt.uint8
ALU = mybir.AluOpType
AF = mybir.ActivationFunctionType
AX = mybir.AxisListType

D = 1024
C2 = 2048
NST = 16
TV = 4096
TH = 2048
TA = 256
NCH = TV // TA
NRE = TH // TA
TC = 512
NCC = TH // TC
CT = C2 // 128
DT = D // 128
NEXP = 16
EPS = 1e-5


class _Chained:
    def then_inc(self, *a):
        return self


_CHAINED = _Chained()


class Ctr:
    def __init__(self, es, nc, name):
        self.sem = es.enter_context(nc.semaphore(name))
        self.n = 0

    def inc(self, inst, k=1):
        if inst is _CHAINED:
            return inst
        if k:
            inst.then_inc(self.sem, k)
            self.n += k
        return inst


_CHAIN_OPS = {
    "tensor_copy", "tensor_tensor", "tensor_add", "tensor_sub", "tensor_mul",
    "tensor_max", "tensor_scalar", "tensor_single_scalar", "tensor_scalar_add",
    "tensor_scalar_sub", "tensor_scalar_mul", "tensor_scalar_max",
    "tensor_scalar_min", "tensor_relu", "tensor_reduce", "reduce_max",
    "reduce_sum", "scalar_tensor_tensor", "tensor_tensor_scan", "memset",
    "select", "copy_predicated", "reciprocal", "activation", "copy", "mul", "add", "square",
    "sqrt", "sign",
}


class DmaQ:

    def __init__(self, eng, ctr):
        self._e = eng
        self._c = ctr

    def wait_ge(self, sem, v):
        return self._e.wait_ge(sem, v)

    def dma_start(self, *a, **k):
        self._e.wait_ge(self._c.sem, self._c.n)
        inst = self._e.dma_start(*a, **k)
        inst.then_inc(self._c.sem, 16)
        self._c.n += 16
        return _CHAINED

    def __getattr__(self, name):
        return getattr(self._e, name)


class Rot:

    def __init__(self, es, nc, name, depth=4):
        self.sems = [es.enter_context(nc.semaphore(f"{name}{i}"))
                     for i in range(depth)]
        self.ns = [0] * depth
        self.i = -1

    def begin(self):
        self.i = (self.i + 1) % len(self.sems)

    def dma(self, eng, **k):
        eng.dma_start(**k).then_inc(self.sems[self.i], 16)
        self.ns[self.i] += 16

    def tok(self):
        return (self.sems[self.i], self.ns[self.i])


class Chain:

    def __init__(self, eng, ctr):
        self._e = eng
        self._c = ctr

    def wait_ge(self, sem, v):
        return self._e.wait_ge(sem, v)

    def dma_start(self, *a, **k):
        return self._e.dma_start(*a, **k)

    def __getattr__(self, name):
        fn = getattr(self._e, name)
        if name in _CHAIN_OPS:
            def wrap(*a, **k):
                self._e.wait_ge(self._c.sem, self._c.n)
                inst = fn(*a, **k)
                inst.then_inc(self._c.sem, 1)
                self._c.n += 1
                return _CHAINED
            return wrap
        return fn


def build_nc(debug=False, phases=5):
    nc = bass.Bass("TRN2", target_bir_lowering=False)
    es = contextlib.ExitStack()
    nc._es = es

    dm = lambda n, s, t, k="ExternalInput": nc.dram_tensor(n, s, t, kind=k).ap()
    xT = dm("xT", [D, TV], F32)
    flag = dm("flag", [1, 1], F32)
    W_in = dm("W_in", [D, 2 * C2], F32)
    convd = dm("convd", [96, 128, 128], F32)
    conv_b = dm("conv_b", [C2], F32)
    W_xproj = dm("W_xproj", [C2, 2 * NST], F32)
    W_dt = dm("W_dt", [C2, C2], F32)
    b_dt = dm("b_dt", [C2], F32)
    A_log = dm("A_log", [NST], F32)
    D_param = dm("D_param", [C2], F32)
    W_out = dm("W_out", [C2, D], F32)
    centT = dm("centT", [D, NST], F32)
    eb = dm("eb", [NST], F32)
    WrT = dm("WrT", [NEXP, D, D], F32)
    br = dm("br", [NEXP, D], F32)
    WsT = dm("WsT", [2, D, D], F32)
    bs = dm("bs", [2, D], F32)
    id16f = dm("id16f", [16, 16], F32)
    id128f = dm("id128f", [128, 128], F32)

    spk = "ExternalOutput" if debug else "Internal"
    u_sp = dm("u_sp", [NCH, 128, CT * TA], F32, spk)
    xib_sp = dm("xib_sp", [NCH, 128, CT * TA], BF16, spk)
    sres_sp = dm("sres_sp", [NCH, 128, CT * TA], F32, spk)
    d_sp = dm("d_sp", [NCH, 128, CT * TA], F32, spk)
    g_sp = dm("g_sp", [NCH, 128, CT * TA], BF16, spk)
    u2_sp = dm("u2_sp", [NCC, 128, DT * TC], F32, spk)
    bc_hbm = dm("bc_hbm", [32, TV], F32, spk)
    ffnin_sp = dm("ffnin_sp", [NCC, 128, DT * TC], F32, spk)
    gates_hbm = dm("gates_hbm", [16, TH], F32, spk)
    oT = dm("oT", [D, TH], F32, "ExternalOutput")

    sb = lambda n, s, t: es.enter_context(nc.sbuf_tensor(n, s, t))
    BC_buf = sb("BC_buf", [32, TV], F32)
    A_bc = sb("A_bc", [128, NST], F32)
    A_row = sb("A_row", [1, NST], F32)
    flag_col = sb("flag_col", [128, 1], F32)
    bdt_c = sb("bdt_c", [128, CT], F32)
    cvb_c = sb("cvb_c", [128, CT], F32)
    Dp_c = sb("Dp_c", [128, CT], F32)
    bsum_c = sb("bsum_c", [128, DT], F32)
    bs_t = sb("bs_t", [128, DT], F32)
    eb_col = sb("eb_col", [16, 1], F32)
    sid16 = sb("sid16", [16, 16], F32)
    sid128 = sb("sid128", [128, 128], F32)
    ones1 = sb("ones1", [128, 1], F32)
    onesr = sb("onesr", [1, 128], F32)
    onesb = sb("onesb", [128, TA], BF16)
    neginf = sb("neginf", [128, NST], F32)
    carry = sb("carry", [128, NST * CT], F32)
    scent = sb("scent", [128, DT * NST], BF16)
    u_b = sb("u_b", [128, CT * TA], F32)
    sres_b = sb("sres_b", [128, CT * TA], F32)
    d_b = sb("d_b", [128, CT * TA], F32)
    du_b = sb("du_b", [128, CT * TA], F32)

    PP = es.enter_context(nc.psum_tensor("PP", [128, 4096], F32))
    pb = [PP[:, i * 512:(i + 1) * 512] for i in range(8)]

    cL = Ctr(es, nc, "cL")
    cS = Ctr(es, nc, "cS")
    cT = Ctr(es, nc, "cT")
    cA = Ctr(es, nc, "cA")
    cV = Ctr(es, nc, "cV")
    cPh = Ctr(es, nc, "cPh")
    rA = Rot(es, nc, "rA")
    rB = Rot(es, nc, "rB")
    rSl = Rot(es, nc, "rSl")
    rY = Rot(es, nc, "rY")
    rC = Rot(es, nc, "rC")
    rDu = Rot(es, nc, "rDu")
    rBb = Rot(es, nc, "rBb")
    ROTS = [rA, rB, rSl, rY, rC, rDu, rBb]

    blk = es.enter_context(nc.Block())

    def emit(_):
        te = nc.tensor
        ve = Chain(nc.vector, cV)
        se = Chain(nc.scalar, cA)
        ge = DmaQ(nc.gpsimd, cL)
        sy = DmaQ(nc.sync, cS)
        geE = nc.gpsimd
        syE = nc.sync

        pool_state = [None] * 8
        pool_rr = [0]

        def acquire_bank():
            i = pool_rr[0] % 8
            pool_rr[0] += 1
            w = pool_state[i]
            if w is not None:
                te.wait_ge(w[0].sem, w[1])
            pool_state[i] = None
            return i

        def release_bank(i, ctr):
            pool_state[i] = (ctr, ctr.n)

        def acquire2():
            if pool_rr[0] % 2 == 1:
                pool_rr[0] += 1
            i = pool_rr[0] % 8
            pool_rr[0] += 2
            for j in (i, i + 1):
                w = pool_state[j]
                if w is not None:
                    te.wait_ge(w[0].sem, w[1])
                pool_state[j] = None
            return i

        def release2(i, ctr):
            pool_state[i] = (ctr, ctr.n)
            pool_state[i + 1] = (ctr, ctr.n)

        def pool_reset():
            for i in range(8):
                w = pool_state[i]
                if w is not None:
                    te.wait_ge(w[0].sem, w[1])
                pool_state[i] = None
            pool_rr[0] = 0

        def pairview(i, w):
            return PP[:, i * 512:(i + 2) * 512].rearrange(
                "p (b n) -> p b n", n=512)[:, :, 0:w]

        def ph_barrier():
            for eng in (te, ve, se, ge, sy):
                for ctr in (cT, cV, cA, cL, cS):
                    eng.wait_ge(ctr.sem, ctr.n)
                for rot in ROTS:
                    for sem, n in zip(rot.sems, rot.ns):
                        if n:
                            eng.wait_ge(sem, n)

        def emit_ln(in_b, out_b, W, sq_b, stat, mu_bc, rs_bc):
            i3 = in_b.rearrange("p (d t) -> p d t", t=W)
            se.wait_ge(cV.sem, cV.n)
            se.wait_ge(cS.sem, cS.n)
            se.wait_ge(cT.sem, cT.n)
            cA.inc(se.activation(out=sq_b[:, 0:DT * W], in_=in_b, func=AF.Square))
            sq_A = cA.n
            b1 = acquire_bank()
            b2 = acquire_bank()
            te.wait_ge(cA.sem, sq_A)
            for dt_i in range(DT):
                cT.inc(te.matmul(pb[b1][0:1, 0:W], ones1[:], i3[:, dt_i, :],
                                 start=(dt_i == 0), stop=(dt_i == DT - 1)), 1 if dt_i == DT - 1 else 0)
            sx_T = cT.n
            sq3 = sq_b[:, 0:DT * W].rearrange("p (d t) -> p d t", t=W)
            for dt_i in range(DT):
                cT.inc(te.matmul(pb[b2][0:1, 0:W], ones1[:], sq3[:, dt_i, :],
                                 start=(dt_i == 0), stop=(dt_i == DT - 1)), 1 if dt_i == DT - 1 else 0)
            sq_T = cT.n
            ve.wait_ge(cT.sem, sq_T)
            cV.inc(ve.tensor_scalar_mul(stat[0:1, 0 * TC:0 * TC + W], pb[b1][0:1, 0:W], 1.0 / D))
            cV.inc(ve.tensor_scalar_mul(stat[0:1, 1 * TC:1 * TC + W], pb[b2][0:1, 0:W], 1.0 / D))
            release_bank(b1, cV)
            release_bank(b2, cV)
            cV.inc(ve.tensor_mul(stat[0:1, 2 * TC:2 * TC + W], stat[0:1, 0 * TC:0 * TC + W], stat[0:1, 0 * TC:0 * TC + W]))
            cV.inc(ve.tensor_sub(stat[0:1, 1 * TC:1 * TC + W], stat[0:1, 1 * TC:1 * TC + W], stat[0:1, 2 * TC:2 * TC + W]))
            cV.inc(ve.tensor_scalar_add(stat[0:1, 1 * TC:1 * TC + W], stat[0:1, 1 * TC:1 * TC + W], EPS))
            cV.inc(ve.reciprocal(stat[0:1, 2 * TC:2 * TC + W], stat[0:1, 1 * TC:1 * TC + W]))
            rec_V = cV.n
            se.wait_ge(cV.sem, rec_V)
            cA.inc(se.activation(out=stat[0:1, 3 * TC:3 * TC + W], in_=stat[0:1, 2 * TC:2 * TC + W], func=AF.Sqrt))
            ve.wait_ge(cA.sem, cA.n)
            cV.inc(ve.tensor_mul(stat[0:1, 4 * TC:4 * TC + W], stat[0:1, 3 * TC:3 * TC + W], stat[0:1, 3 * TC:3 * TC + W]))
            cV.inc(ve.tensor_mul(stat[0:1, 4 * TC:4 * TC + W], stat[0:1, 4 * TC:4 * TC + W], stat[0:1, 1 * TC:1 * TC + W]))
            cV.inc(ve.tensor_scalar(stat[0:1, 4 * TC:4 * TC + W], stat[0:1, 4 * TC:4 * TC + W], -1.0, 3.0,
                                    ALU.mult, ALU.add))
            cV.inc(ve.tensor_mul(stat[0:1, 4 * TC:4 * TC + W], stat[0:1, 4 * TC:4 * TC + W], stat[0:1, 3 * TC:3 * TC + W]))
            cV.inc(ve.tensor_scalar_mul(stat[0:1, 4 * TC:4 * TC + W], stat[0:1, 4 * TC:4 * TC + W], 0.5))
            st_V = cV.n
            te.wait_ge(cV.sem, st_V)
            cT.inc(te.matmul(pb[b1][:, 0:W], onesr[:], stat[0:1, 0 * TC:0 * TC + W],
                             start=True, stop=True), 1)
            cT.inc(te.matmul(pb[b2][:, 0:W], onesr[:], stat[0:1, 4 * TC:4 * TC + W],
                             start=True, stop=True), 1)
            bc_T = cT.n
            ve.wait_ge(cT.sem, bc_T)
            mu3 = pb[b1][:, 0:W].rearrange("p (o t) -> p o t", o=1).broadcast_to([128, DT, W])
            rs3 = pb[b2][:, 0:W].rearrange("p (o t) -> p o t", o=1).broadcast_to([128, DT, W])
            o3 = out_b.rearrange("p (d t) -> p d t", t=W)
            cV.inc(ve.tensor_tensor(o3, i3, mu3, ALU.subtract))
            cV.inc(ve.tensor_tensor(o3, o3, rs3, ALU.mult))
            release_bank(b1, cV)
            release_bank(b2, cV)
            return cV.n

        esA = contextlib.ExitStack()
        sbA = lambda n, s, t: esA.enter_context(nc.sbuf_tensor(n, s, t))
        wi = sbA("wi", [128, 8 * C2], BF16)
        wres = sbA("wres", [128, 8 * C2], BF16)
        cvd = sbA("cvd", [128, 96 * 128], BF16)
        wxp = sbA("wxp", [128, CT * 32], BF16)
        xib_b = sbA("xib_b", [128, CT * TA], BF16)
        cL.inc(ge.dma_start(out=wi[:].rearrange("p (k c) -> p k c", c=C2),
                            in_=W_in.rearrange("(k p) c -> p k c", p=128)[:, :, 0:C2]), 16)
        cL.inc(ge.dma_start(out=wres[:].rearrange("p (k c) -> p k c", c=C2),
                            in_=W_in.rearrange("(k p) c -> p k c", p=128)[:, :, C2:2 * C2]), 16)
        cL.inc(ge.dma_start(out=cvd[:].rearrange("p (i n) -> p i n", n=128),
                            in_=convd.rearrange("i p n -> p i n")), 16)
        cL.inc(ge.dma_start(out=wxp[:].rearrange("p (k c) -> p k c", c=32),
                            in_=W_xproj.rearrange("(k p) c -> p k c", p=128)), 16)
        wload = cL.n
        cS.inc(sy.dma_start(out=A_row[:], in_=A_log.rearrange("(o n) -> o n", o=1)), 16)
        cS.inc(sy.dma_start(out=flag_col[0:1, :], in_=flag), 16)
        cS.inc(sy.dma_start(out=bdt_c[:],
                            in_=b_dt.rearrange("(c p) -> p c", p=128),
                            allow_slow_non_contiguous=True), 16)
        cS.inc(sy.dma_start(out=cvb_c[:],
                            in_=conv_b.rearrange("(c p) -> p c", p=128),
                            allow_slow_non_contiguous=True), 16)
        cS.inc(sy.dma_start(out=Dp_c[:],
                            in_=D_param.rearrange("(c p) -> p c", p=128),
                            allow_slow_non_contiguous=True), 16)
        cS.inc(sy.dma_start(out=bsum_c[:],
                            in_=bs[0].rearrange("(c p) -> p c", p=128),
                            allow_slow_non_contiguous=True), 16)
        cS.inc(sy.dma_start(out=bs_t[:],
                            in_=bs[1].rearrange("(c p) -> p c", p=128),
                            allow_slow_non_contiguous=True), 16)
        cS.inc(sy.dma_start(out=eb_col[:], in_=eb.rearrange("(n o) -> n o", o=1)), 16)
        cS.inc(sy.dma_start(out=sid16[:], in_=id16f), 16)
        cS.inc(sy.dma_start(out=sid128[:], in_=id128f), 16)
        cL.inc(ge.dma_start(out=scent[:].rearrange("p (k c) -> p k c", c=NST),
                            in_=centT.rearrange("(k p) c -> p k c", p=128)), 16)
        setup_S = cS.n
        se.wait_ge(cS.sem, setup_S)
        cA.inc(se.activation(out=A_row[:], in_=A_row[:], func=AF.Exp))
        ve.wait_ge(cA.sem, cA.n)
        ve.wait_ge(cS.sem, setup_S)
        cV.inc(ve.tensor_scalar_mul(A_row[:], A_row[:], -1.0))
        cV.inc(ve.tensor_add(bsum_c[:], bsum_c[:], bs_t[:]))
        cV.inc(ve.tensor_scalar_mul(bsum_c[:], bsum_c[:], 0.5))
        cV.inc(ve.memset(neginf[:], -3.0e38))
        cV.inc(ve.memset(ones1[:], 1.0))
        cV.inc(ve.memset(onesr[:], 1.0))
        cV.inc(ve.memset(onesb[:], 1.0))
        cV.inc(ve.memset(carry[:], 0.0))
        sy.wait_ge(cV.sem, cV.n)
        cS.inc(sy.dma_start(out=A_bc[0:1, :], in_=A_row[:]), 16)
        for k in [1, 2, 4, 8, 16, 32, 64]:
            sy.wait_ge(cS.sem, cS.n)
            cS.inc(sy.dma_start(out=A_bc[k:2 * k, :], in_=A_bc[0:k, :]), 16)
        for k in [1, 2, 4, 8, 16, 32, 64]:
            sy.wait_ge(cS.sem, cS.n)
            cS.inc(sy.dma_start(out=flag_col[k:2 * k, :], in_=flag_col[0:k, :]), 16)
        consts_S = cS.n

        xcb = [sbA(f"xcb{i}", [128, 8 * TA], BF16) for i in range(2)]
        xiraw = sbA("xiraw", [128, CT * (TA + 3)], BF16)
        xr3 = xiraw[:].rearrange("p (c n) -> p c n", n=TA + 3)
        cV.inc(ve.memset(xr3[:, :, 0:3], 0.0))
        halo_v = cV.n
        xcons = [0, 0]
        spill_tok = None
        conv_T = 0
        se.wait_ge(cS.sem, consts_S)
        ve.wait_ge(cS.sem, consts_S)
        for ch in range(NCH):
            bi = ch % 2
            if xcons[bi]:
                ge.wait_ge(cT.sem, xcons[bi])
            cL.inc(ge.dma_start(
                out=xcb[bi][:].rearrange("p (k t) -> p k t", t=TA),
                in_=xT.rearrange("(k p) t -> p k t", p=128)[:, :, ch * TA:(ch + 1) * TA]), 16)
            te.wait_ge(cL.sem, max(cL.n, wload))
            ve.wait_ge(cV.sem, halo_v)
            if spill_tok:
                ve.wait_ge(*spill_tok)
            if ch:
                ve.wait_ge(cT.sem, conv_T)
            for mp in range(CT // 2):
                b = acquire2()
                for mt in (2 * mp, 2 * mp + 1):
                    bb = b + (mt & 1)
                    for kt in range(8):
                        cT.inc(te.matmul(pb[bb][:, 0:TA],
                                         wi[:, kt * C2 + mt * 128: kt * C2 + (mt + 1) * 128],
                                         xcb[bi][:, kt * TA:(kt + 1) * TA],
                                         start=(kt == 0), stop=(kt == 7)), 1 if kt == 7 else 0)
                ve.wait_ge(cT.sem, cT.n)
                cV.inc(ve.tensor_copy(xr3[:, 2 * mp:2 * mp + 2, 3:3 + TA],
                                      pairview(b, TA)))
                release2(b, cV)
            xi_raw_A = cV.n
            te.wait_ge(cV.sem, xi_raw_A)
            for cp in range(CT // 2):
                b = acquire2()
                for ct in (2 * cp, 2 * cp + 1):
                    bb = b + (ct & 1)
                    for tap in range(4):
                        cT.inc(te.matmul(pb[bb][:, 0:TA],
                                         cvd[:, (tap * 16 + ct) * 128:(tap * 16 + ct + 1) * 128],
                                         xr3[:, ct, tap:tap + TA],
                                         start=(tap == 0), stop=False), 0)
                    cT.inc(te.matmul(pb[bb][:, 0:TA],
                                     cvd[:, (64 + ct) * 128:(64 + ct + 1) * 128],
                                     onesb[:, 0:TA], start=False, stop=True), 1)
                se.wait_ge(cT.sem, cT.n)
                cA.inc(se.activation(
                    out=u_b[:, 2 * cp * TA:(2 * cp + 2) * TA].rearrange(
                        "p (b n) -> p b n", n=TA),
                    in_=pairview(b, TA), func=AF.Silu))
                release2(b, cA)
            conv_T = cT.n
            u_A = cA.n
            ve.wait_ge(cT.sem, conv_T)
            cV.inc(ve.tensor_copy(xr3[:, :, 0:3], xr3[:, :, TA:TA + 3]))
            halo_v = cV.n
            ve.wait_ge(cA.sem, u_A)
            if spill_tok:
                ve.wait_ge(*spill_tok)
            cV.inc(ve.tensor_copy(xib_b[:], u_b[:]))
            xib_V = cV.n
            b = acquire_bank()
            te.wait_ge(cV.sem, xib_V)
            for kt in range(CT):
                cT.inc(te.matmul(pb[b][0:32, 0:TA], wxp[:, kt * 32:(kt + 1) * 32],
                                 xib_b[:, kt * TA:(kt + 1) * TA],
                                 start=(kt == 0), stop=(kt == CT - 1)), 1 if kt == CT - 1 else 0)
            se.wait_ge(cT.sem, cT.n)
            cA.inc(se.activation(out=BC_buf[:, ch * TA:(ch + 1) * TA], in_=pb[b][0:32, 0:TA],
                                 func=AF.Copy))
            release_bank(b, cA)
            sres_A = 0
            if ch >= NRE:
                for mp in range(CT // 2):
                    b = acquire2()
                    for mt in (2 * mp, 2 * mp + 1):
                        bb = b + (mt & 1)
                        for kt in range(8):
                            cT.inc(te.matmul(pb[bb][:, 0:TA],
                                             wres[:, kt * C2 + mt * 128: kt * C2 + (mt + 1) * 128],
                                             xcb[bi][:, kt * TA:(kt + 1) * TA],
                                             start=(kt == 0), stop=(kt == 7)), 1 if kt == 7 else 0)
                    se.wait_ge(cT.sem, cT.n)
                    cA.inc(se.activation(
                        out=sres_b[:, 2 * mp * TA:(2 * mp + 2) * TA].rearrange(
                            "p (b n) -> p b n", n=TA),
                        in_=pairview(b, TA), func=AF.Silu))
                    release2(b, cA)
                sres_A = cA.n
            xcons[bi] = cT.n
            rA.begin()
            syE.wait_ge(cA.sem, u_A)
            rA.dma(syE, out=u_sp[ch], in_=u_b[:])
            syE.wait_ge(cV.sem, xib_V)
            rA.dma(syE, out=xib_sp[ch], in_=xib_b[:])
            if ch >= NRE:
                syE.wait_ge(cA.sem, sres_A)
                rA.dma(syE, out=sres_sp[ch], in_=sres_b[:])
            spill_tok = rA.tok()

        sy.wait_ge(cA.sem, cA.n)
        cS.inc(sy.dma_start(out=bc_hbm, in_=BC_buf[:]), 16)
        ph_barrier()
        esA.close()
        if phases < 2:
            return

        esB = contextlib.ExitStack()
        sbB = lambda n, s, t: esB.enter_context(nc.sbuf_tensor(n, s, t))
        TAp = TA + 1
        dA_b = [sbB(f"dAb{i}", [128, CT * TAp], F32) for i in range(2)]
        dBu_b = sbB("dBu", [128, CT * TAp], F32)
        h_b = sbB("h_b", [128, CT * TAp], F32)
        y_acc = sbB("y_acc", [128, CT * TA], F32)
        Bb_b = sbB("Bb_b", [128, NST * TA], F32)
        Cb_b = sbB("Cb_b", [128, NST * TA], F32)
        Bbc = Bb_b[:]
        Cbc = Cb_b[:]

        def bchain(eng, buf, r0, win):
            rBb.begin()
            rBb.dma(eng,
                    out=buf[:].rearrange("p (n t) -> p n t", n=NST),
                    in_=bc_hbm[r0:r0 + NST, win * TA:(win + 1) * TA][None]
                    .broadcast_to([128, NST, TA]))
        g_b = sbB("g_b", [128, CT * TA], BF16)
        dA3 = [dA_b[i][:].rearrange("p (c n) -> p c n", n=TAp) for i in range(2)]
        dBu3 = dBu_b[:].rearrange("p (c n) -> p c n", n=TAp)
        h3 = h_b[:].rearrange("p (c n) -> p c n", n=TAp)
        d3 = d_b[:].rearrange("p (c t) -> p c t", t=TA)
        du3 = du_b[:].rearrange("p (c t) -> p c t", t=TA)
        y3 = y_acc[:].rearrange("p (c t) -> p c t", t=TA)
        ve.wait_ge(cS.sem, consts_S)
        for i in range(2):
            cV.inc(ve.memset(dA3[i][:, :, 0:1], 0.0))
        pad_V = cV.n
        pool_reset()
        te.wait_ge(cS.sem, consts_S)

        wdt_sl = Cb_b[:].bitcast(BF16)
        xcv = y_acc[:].bitcast(BF16)
        xib2 = xcv[:, 0:CT * TA]
        e1p = g_b[:].bitcast(F32)
        d2_b = sres_b
        cL.inc(ge.dma_start(
            out=xcv[:, CT * TA:CT * TA + 2048].rearrange("p (i n) -> p i n", n=128),
            in_=convd.rearrange("i p n -> p i n")[:, 80:96, :]), 16)
        cvd2_L = cL.n
        WdtR = W_dt.rearrange("(k p) c -> p k c", p=128)
        a2st = dict(mmend=0, slfree=[0, 0], spar=0, spill=0, lastA=0)
        a2_spill = [0] * NCH
        a2evacs = []

        def a2_group(k, mp):
            if mp == 0:
                sy.wait_ge(cT.sem, a2st['mmend'])
                cS.inc(sy.dma_start(out=xib2, in_=xib_sp[k]), 16)
                a2st['xload'] = cS.n
            sl = a2st['spar']
            a2st['spar'] ^= 1
            slab = wdt_sl[:, sl * 4096:(sl + 1) * 4096]
            rSl.begin()
            geE.wait_ge(cT.sem, a2st['slfree'][sl])
            rSl.dma(geE,
                    out=slab.rearrange("p (j c) -> p j c", c=256),
                    in_=WdtR[:, :, 2 * mp * 128:(2 * mp + 2) * 128])
            te.wait_ge(*rSl.tok())
            if mp == 0:
                te.wait_ge(cS.sem, a2st['xload'])
                te.wait_ge(cL.sem, cvd2_L)
            b = acquire2()
            for i in (0, 1):
                mt = 2 * mp + i
                bb = b + i
                for kt in range(CT):
                    cT.inc(te.matmul(pb[bb][:, 0:TA],
                                     slab[:, kt * 256 + i * 128: kt * 256 + (i + 1) * 128],
                                     xib2[:, kt * TA:(kt + 1) * TA],
                                     start=(kt == 0), stop=False), 0)
                cT.inc(te.matmul(pb[bb][:, 0:TA],
                                 xcv[:, CT * TA + mt * 128:CT * TA + (mt + 1) * 128],
                                 onesb[:, 0:TA], start=False, stop=True), 1)
            a2st['slfree'][sl] = cT.n
            if mp == CT // 2 - 1:
                a2st['mmend'] = cT.n
            a2evacs.append((b, k, mp, cT.n))

        def a2_evac(b, k, mp, mmT):
            se.wait_ge(cT.sem, mmT)
            cA.inc(se.activation(
                out=e1p[:, 0:2 * TA].rearrange("p (b n) -> p b n", n=TA),
                in_=pairview(b, TA), func=AF.Exp))
            if mp == 0 and a2st['spill']:
                se.wait_ge(cL.sem, a2st['spill'])
            cA.inc(se.activation(
                out=d2_b[:, 2 * mp * TA:(2 * mp + 2) * TA].rearrange(
                    "p (b n) -> p b n", n=TA),
                in_=e1p[:, 0:2 * TA].rearrange("p (b n) -> p b n", n=TA),
                func=AF.Ln, bias=1.0))
            release2(b, cA)
            a2st['lastA'] = cA.n
            if mp == CT // 2 - 1:
                ge.wait_ge(cA.sem, cA.n)
                cL.inc(ge.dma_start(out=d_sp[k], in_=d2_b[:]), 16)
                a2st['spill'] = cL.n
                a2_spill[k] = cL.n

        for mp in range(CT // 2):
            if len(a2evacs) >= 4:
                a2_evac(*a2evacs.pop(0))
            a2_group(0, mp)
        while a2evacs:
            a2_evac(*a2evacs.pop(0))
        rB.begin()
        syE.wait_ge(cL.sem, cL.n)
        rB.dma(syE, out=d_b[:], in_=d_sp[0])
        ev_next = rB.tok()
        bchain(syE, Bb_b, 0, 0)
        bb_next = rBb.tok()
        rDu.begin()
        geE.wait_ge(cL.sem, cL.n)
        rDu.dma(geE, out=du_b[:], in_=u_sp[0])
        du_next = rDu.tok()
        for mp in range(CT // 2):
            if len(a2evacs) >= 4:
                a2_evac(*a2evacs.pop(0))
            a2_group(1, mp)
        while a2evacs:
            a2_evac(*a2evacs.pop(0))

        zmm_T = 0
        prev_scan = 0
        dfree_A = 0
        dufree_V = 0
        dumult_V = 0
        gspill_L = 0
        yld_next = None
        for ch in range(NCH):
            a2_done = 0
            real = ch >= NRE
            ev_cur = ev_next
            if ch <= NRE - 2:
                a2_sched = [(2 * ch + 2, mp) for mp in range(CT // 2)] + \
                           [(2 * ch + 3, mp) for mp in range(CT // 2)]
                a2_target = [0, 0, 0, 0, 1, 2, 3, 4, 5, 6, 7, 8, 10, 12, 14, 15]
            else:
                a2_sched = None
            if ch == NRE:
                zmm_T = a2st['mmend']
                ve.wait_ge(cS.sem, consts_S)
                ve.wait_ge(cA.sem, cA.n)
                cV.inc(ve.tensor_scalar_mul(carry[:], carry[:], flag_col[:]))
                prev_scan = cV.n
            du_cur = du_next
            bb_cur = bb_next
            if real:
                cb_cur = cb_next
            ve.wait_ge(*ev_cur)
            ve.wait_ge(*du_cur)
            cV.inc(ve.tensor_mul(du_b[:], d_b[:], du_b[:]))
            dumult_V = cV.n
            se.wait_ge(*ev_cur)
            se.wait_ge(cV.sem, pad_V)
            if ch + 1 < NCH:
                rB.begin()
            dA_used = [0, 0]

            def emit_dA(n):
                pi = n % 2
                if dA_used[pi]:
                    se.wait_ge(cV.sem, dA_used[pi])
                cA.inc(se.activation(out=dA3[pi][:, :, 1:TAp], in_=d3,
                                     func=AF.Exp, scale=A_bc[:, n:n + 1]))
                return cA.n

            def emit_pad(n):
                se.wait_ge(cV.sem, prev_scan)
                cA.inc(se.activation(
                    out=dBu3[:, :, 0:1],
                    in_=carry[:, n * CT:(n + 1) * CT].rearrange("p (c o) -> p c o", o=1),
                    func=AF.Copy))
                return cA.n

            emit_dA(0)
            actn = emit_pad(0)
            for n in range(NST):
                if a2_sched is not None:
                    while a2_done < a2_target[n]:
                        a2_evac(*a2evacs.pop(0))
                        a2_done += 1
                    a2_group(*a2_sched[n])
                pi = n % 2
                bview = Bbc[:, n * TA:(n + 1) * TA].rearrange(
                    "p (o t) -> p o t", o=1).broadcast_to([128, CT, TA])
                if n == 0:
                    ve.wait_ge(*bb_cur)
                cV.inc(ve.tensor_tensor(dBu3[:, :, 1:TAp], du3, bview, ALU.mult))
                if n == NST - 1:
                    dufree_V = cV.n
                    if ch + 1 < NCH:
                        syE.wait_ge(cL.sem, a2_spill[ch + 1])
                        syE.wait_ge(cA.sem, dfree_A)
                        syE.wait_ge(cV.sem, dumult_V)
                        rB.dma(syE, out=d_b[:], in_=d_sp[ch + 1])
                        syE.wait_ge(cV.sem, dufree_V)
                        bchain(syE, Bb_b, 0, ch + 1)
                        bb_next = rBb.tok()
                        rDu.begin()
                        geE.wait_ge(cV.sem, dufree_V)
                        rDu.dma(geE, out=du_b[:], in_=u_sp[ch + 1])
                        du_next = rDu.tok()
                ve.wait_ge(cA.sem, actn)
                cV.inc(ve.tensor_tensor_scan(out=h_b[:], data0=dA_b[pi][:],
                                             data1=dBu_b[:], initial=0.0,
                                             op0=ALU.mult, op1=ALU.add))
                scan_V = cV.n
                prev_scan = scan_V
                dA_used[pi] = cV.n
                if n + 1 < NST:
                    dn = emit_dA(n + 1)
                    if n + 1 == NST - 1:
                        dfree_A = dn
                se.wait_ge(cV.sem, scan_V)
                cA.inc(se.activation(
                    out=carry[:, n * CT:(n + 1) * CT].rearrange("p (c o) -> p c o", o=1),
                    in_=h3[:, :, TA:TAp], func=AF.Copy))
                if n + 1 < NST:
                    actn = emit_pad(n + 1)
                if real:
                    cview = Cbc[:, n * TA:(n + 1) * TA].rearrange(
                        "p (o t) -> p o t", o=1).broadcast_to([128, CT, TA])
                    if zmm_T:
                        ve.wait_ge(cT.sem, zmm_T)
                    if n == 0:
                        ve.wait_ge(*cb_cur)
                    cV.inc(ve.tensor_tensor(y3, h3[:, :, 1:TAp], cview, ALU.mult))
                    if ch == NRE and n == 0:
                        te.wait_ge(cA.sem, a2st['lastA'])
                    te.wait_ge(cV.sem, cV.n)
                    for zb in range(8):
                        cT.inc(te.matmul(
                            PP[:, zb * 512:(zb + 1) * 512],
                            sid128[:], y_acc[:, zb * 512:(zb + 1) * 512],
                            start=(n == 0), stop=(n == NST - 1)),
                               1 if zb == 7 else 0)
                    zmm_T = cT.n
            while a2evacs:
                a2_evac(*a2evacs.pop(0))
            if ch + 1 < NCH:
                if ch + 1 == NRE:
                    rY.begin()
                    syE.wait_ge(cL.sem, a2st['spill'])
                    rY.dma(syE, out=sres_b[:], in_=sres_sp[NRE])
                    rY.dma(syE, out=u_b[:], in_=u_sp[NRE])
                    yld_next = rY.tok()
                if ch + 1 >= NRE:
                    rC.begin()
                    if ch + 1 == NRE:
                        geE.wait_ge(cT.sem, a2st['mmend'])
                    geE.wait_ge(cV.sem, cV.n if real else dufree_V)
                    rC.dma(geE,
                           out=Cb_b[:].rearrange("p (n t) -> p n t", n=NST),
                           in_=bc_hbm[NST:2 * NST,
                                      (ch + 1) * TA:(ch + 2) * TA][None]
                           .broadcast_to([128, NST, TA]))
                    cb_next = rC.tok()
            if real:
                yld_cur = yld_next
                ve.wait_ge(cT.sem, zmm_T)
                ve.wait_ge(*yld_cur)
                for ct in range(CT):
                    cV.inc(ve.scalar_tensor_tensor(
                        out=y_acc[:, ct * TA:(ct + 1) * TA],
                        in0=u_b[:, ct * TA:(ct + 1) * TA],
                        scalar=Dp_c[:, ct:ct + 1],
                        in1=PP[:, ct * TA:(ct + 1) * TA],
                        op0=ALU.mult, op1=ALU.add))
                if ch == NRE:
                    ve.wait_ge(cA.sem, a2st['lastA'])
                ve.wait_ge(cL.sem, gspill_L)
                cV.inc(ve.tensor_mul(g_b[:], y_acc[:], sres_b[:]))
                ge.wait_ge(cV.sem, cV.n)
                cL.inc(ge.dma_start(out=g_sp[ch], in_=g_b[:]), 16)
                gspill_L = cL.n
                if ch + 1 < NCH:
                    rY.begin()
                    syE.wait_ge(cV.sem, cV.n)
                    rY.dma(syE, out=u_b[:], in_=u_sp[ch + 1])
                    rY.dma(syE, out=sres_b[:], in_=sres_sp[ch + 1])
                    yld_next = rY.tok()
            if ch + 1 < NCH:
                ev_next = rB.tok()

        ph_barrier()
        esB.close()
        if phases < 4:
            return

        u2_b = sb("u2_b", [128, DT * TC], F32)
        u2b_b = sb("u2b_b", [128, DT * TC], BF16)
        gates_r = sb("gates_r", [16, TH], F32)
        gates_rb = sb("gates_rb", [16, TH], BF16)
        stat = sb("stat", [1, 5 * TC], F32)
        mu_bc = sb("mu_bc", [128, TC], F32)
        rs_bc = sb("rs_bc", [128, TC], F32)
        esC = contextlib.ExitStack()
        sbC = lambda n, s, t: esC.enter_context(nc.sbuf_tensor(n, s, t))
        wout = sbC("wout", [128, CT * D], BF16)
        gbig = sbC("gbig", [128, CT * TC], BF16)
        resid2 = sbC("resid2", [128, DT * TC], F32)
        sc_b = sbC("sc_b", [16, TC], F32)
        sT_b = sbC("sT_b", [128, 4 * NST], F32)
        m_b = sbC("m_b", [128, 8], F32)
        msk_b = sbC("msk_b", [128, NST], U8)
        gT_b = sbC("gT_b", [128, 4 * NST], F32)
        cL.inc(ge.dma_start(out=wout[:].rearrange("p (k c) -> p k c", c=D),
                            in_=W_out.rearrange("(k p) c -> p k c", p=128)), 16)
        wout_L = cL.n
        te.wait_ge(cS.sem, consts_S)
        ve.wait_ge(cS.sem, consts_S)
        xc_par = [d_b, du_b]
        gb_par = [gbig[:], sres_b[:].bitcast(BF16)[:, 0:CT * TC]]
        rs_par = [u_b, resid2]
        ldc_tok = [None, None]
        mmend_par = [0, 0]
        evacV_par = [0, 0]

        def c_loads(cc):
            par = cc % 2
            rB.begin()
            syE.wait_ge(cT.sem, mmend_par[par])
            gv3 = gb_par[par].rearrange("p (c t) -> p c t", t=TC)
            rB.dma(syE, out=gv3[:, :, 0:TA],
                   in_=g_sp[NRE + 2 * cc].rearrange("p (c t) -> p c t", t=TA))
            rB.dma(syE, out=gv3[:, :, TA:TC],
                   in_=g_sp[NRE + 2 * cc + 1].rearrange("p (c t) -> p c t", t=TA))
            syE.wait_ge(cV.sem, evacV_par[par])
            rB.dma(syE, out=xc_par[par][:].rearrange("p (k t) -> p k t", t=TC),
                   in_=xT.rearrange("(k p) t -> p k t", p=128)[
                       :, :, TH + cc * TC:TH + (cc + 1) * TC])
            ldc_tok[par] = rB.tok()

        c_loads(0)
        for cc in range(NCC):
            par = cc % 2
            if cc + 1 < NCC:
                c_loads(cc + 1)
            te.wait_ge(*ldc_tok[par])
            te.wait_ge(cL.sem, wout_L)
            ve.wait_ge(*ldc_tok[par])
            for mt in range(DT):
                b = acquire_bank()
                for kt in range(CT):
                    cT.inc(te.matmul(pb[b][:, 0:TC],
                                     wout[:, kt * D + mt * 128: kt * D + (mt + 1) * 128],
                                     gb_par[par][:, kt * TC:(kt + 1) * TC],
                                     start=(kt == 0), stop=(kt == CT - 1)), 1 if kt == CT - 1 else 0)
                ve.wait_ge(cT.sem, cT.n)
                cV.inc(ve.tensor_add(rs_par[par][:, mt * TC:(mt + 1) * TC], pb[b][:, 0:TC],
                                     xc_par[par][:, mt * TC:(mt + 1) * TC]))
                release_bank(b, cV)
            mmend_par[par] = cT.n
            evacV_par[par] = cV.n
            emit_ln(rs_par[par][:], u2_b[:], TC, u2_b[:], stat, mu_bc, rs_bc)
            ve.wait_ge(cT.sem, cT.n)
            cV.inc(ve.tensor_copy(u2b_b[:], u2_b[:]))
            u2_V = cV.n
            sy.wait_ge(cV.sem, u2_V)
            cS.inc(sy.dma_start(out=u2_sp[cc], in_=u2_b[:]), 16)
            b = acquire_bank()
            te.wait_ge(cV.sem, u2_V)
            for kt in range(DT):
                cT.inc(te.matmul(pb[b][0:16, 0:TC], scent[:, kt * NST:(kt + 1) * NST],
                                 u2b_b[:].rearrange("p (k t) -> p k t", t=TC)[:, kt, :],
                                 start=(kt == 0), stop=(kt == DT - 1)), 1 if kt == DT - 1 else 0)
            sc_T = cT.n
            ve.wait_ge(cT.sem, sc_T)
            cV.inc(ve.tensor_scalar(sc_b[0:16, 0:TC], pb[b][0:16, 0:TC],
                                    eb_col[:], None, ALU.add))
            release_bank(b, cV)
            raw_A = cV.n
            bT = acquire_bank()
            te.wait_ge(cV.sem, raw_A)
            for tb in range(4):
                cT.inc(te.transpose(pb[bT][:, tb * NST:(tb + 1) * NST],
                                    sc_b[0:16, tb * 128:(tb + 1) * 128], sid16[:]),
                       1 if tb == 3 else 0)
            ve.wait_ge(cT.sem, cT.n)
            cV.inc(ve.tensor_copy(sT_b[:], pb[bT][:, 0:4 * NST]))
            release_bank(bT, cV)
            sT3 = sT_b[:].rearrange("p (b n) -> p b n", n=NST)
            for tb in range(4):
                cV.inc(ve.reduce_max(m_b[:, tb:tb + 1], sT3[:, tb, :], axis=AX.X))
                cV.inc(ve.tensor_scalar(sT3[:, tb, :], sT3[:, tb, :],
                                        m_b[:, tb:tb + 1], None, ALU.subtract))
            sm_V = cV.n
            se.wait_ge(cV.sem, sm_V)
            cA.inc(se.activation(out=sT_b[:], in_=sT_b[:], func=AF.Exp))
            exp_A = cA.n
            ve.wait_ge(cA.sem, exp_A)
            for tb in range(4):
                cV.inc(ve.reduce_sum(m_b[:, 4 + tb:5 + tb], sT3[:, tb, :], axis=AX.X))
                cV.inc(ve.reciprocal(m_b[:, 4 + tb:5 + tb], m_b[:, 4 + tb:5 + tb]))
                cV.inc(ve.tensor_scalar(sT3[:, tb, :], sT3[:, tb, :],
                                        m_b[:, 4 + tb:5 + tb], None, ALU.mult))
                cV.inc(ve.reduce_max(m_b[:, tb:tb + 1], sT3[:, tb, :], axis=AX.X))
                cV.inc(ve.tensor_scalar(msk_b[:], sT3[:, tb, :], m_b[:, tb:tb + 1],
                                        None, ALU.is_ge))
                cV.inc(ve.tensor_copy(gT_b[:, tb * NST:(tb + 1) * NST], sT3[:, tb, :]))
                cV.inc(ve.copy_predicated(gT_b[:, tb * NST:(tb + 1) * NST], msk_b[:],
                                          neginf[:]))
                cV.inc(ve.reduce_max(m_b[:, tb:tb + 1],
                                     gT_b[:].rearrange("p (b n) -> p b n", n=NST)[:, tb, :],
                                     axis=AX.X))
                cV.inc(ve.tensor_scalar(gT_b[:, tb * NST:(tb + 1) * NST], sT3[:, tb, :],
                                        m_b[:, tb:tb + 1], None, ALU.is_ge))
                cV.inc(ve.tensor_mul(gT_b[:, tb * NST:(tb + 1) * NST],
                                     gT_b[:, tb * NST:(tb + 1) * NST], sT3[:, tb, :]))
            g2_V = cV.n
            bG = acquire_bank()
            te.wait_ge(cV.sem, g2_V)
            for tb in range(4):
                cT.inc(te.transpose(pb[bG][0:16, tb * 128:(tb + 1) * 128],
                                    gT_b[:, tb * NST:(tb + 1) * NST], sid128[:]),
                       1 if tb == 3 else 0)
            se.wait_ge(cT.sem, cT.n)
            cA.inc(se.activation(out=gates_r[:, cc * TC:(cc + 1) * TC],
                                 in_=pb[bG][0:16, 0:TC], func=AF.Copy))
            release_bank(bG, cA)

        cA.inc(se.activation(out=gates_rb[:], in_=gates_r[:], func=AF.Copy))
        sy.wait_ge(cA.sem, cA.n)
        cS.inc(sy.dma_start(out=gates_hbm, in_=gates_r[:]), 16)
        ph_barrier()
        if phases < 5:
            esC.close()
            return

        esC.close()
        esD = contextlib.ExitStack()
        sbD = lambda n, s, t: esD.enter_context(nc.sbuf_tensor(n, s, t))
        wr_b = [sbD(f"wr{i}", [128, DT * D], BF16) for i in range(2)]
        ue_b = [sbD(f"ue{i}", [128, DT * TC], BF16) for i in range(2)]
        gbc2_b = [sbD("gbc_b", [128, 4 * TC], F32),
                  u2b_b[:].bitcast(F32)[:, 0:4 * TC]]
        uh_b = sbD("uh_b", [128, DT * TC], BF16)
        ln2_b = u_b
        out_b = sres_b
        wbr = sbD("wbr", [16, DT * 128], BF16)
        cL.inc(ge.dma_start(out=wbr[:], in_=br), 16)
        wd_L = cL.n
        wr_loaded = [0, 0]
        wr_freeT = [0, 0]
        ue_V = [0, 0]
        ue_freeT = [0, 0]
        gbc_S = [0, 0]
        gbc_cons = [0, 0]

        def gbc_chain(tch, grp):
            par = grp % 2
            buf = gbc2_b[par]
            if gbc_cons[par]:
                sy.wait_ge(cV.sem, gbc_cons[par])
            cS.inc(sy.dma_start(
                out=buf[:].rearrange("p (n t) -> p n t", n=4),
                in_=gates_hbm[4 * grp:4 * grp + 4,
                              tch * TC:(tch + 1) * TC][None]
                .broadcast_to([128, 4, TC])), 16)
            gbc_S[par] = cS.n

        for tch in range(NCC):
            cS.inc(sy.dma_start(out=u2_b[:], in_=u2_sp[tch]), 16)
            u2_S = cS.n
            ve.wait_ge(cS.sem, u2_S)
            ve.wait_ge(cT.sem, cT.n)
            cV.inc(ve.tensor_scalar_mul(uh_b[:], u2_b[:], 0.5))
            base_V = cV.n
            experts = list(range(NEXP)) + [NEXP, NEXP + 1]

            def load_wr(e, wi_):
                if wr_freeT[wi_]:
                    ge.wait_ge(cT.sem, wr_freeT[wi_])
                src_ap = (WrT[e] if e < NEXP else WsT[e - NEXP]).rearrange(
                    "(k p) c -> p k c", p=128)
                cL.inc(ge.dma_start(
                    out=wr_b[wi_][:].rearrange("p (k c) -> p k c", c=D),
                    in_=src_ap), 16)
                wr_loaded[wi_] = cL.n

            load_wr(experts[0], 0)
            load_wr(experts[1], 1)
            gbc_chain(tch, 0)
            gbc_chain(tch, 1)
            banks_d = [acquire_bank() for _ in range(DT)]
            te.wait_ge(cS.sem, u2_S)
            te.wait_ge(cL.sem, wd_L)
            for dt_i in range(DT):
                cT.inc(te.matmul(pb[banks_d[dt_i]][:, 0:TC],
                                 wbr[:, dt_i * 128:(dt_i + 1) * 128],
                                 gates_rb[:, tch * TC:(tch + 1) * TC],
                                 start=True, stop=False),
                       1 if dt_i == DT - 1 else 0)
            for ei, e in enumerate(experts):
                wi_ = ei % 2
                if e < NEXP:
                    par = (e // 4) % 2
                    gv = gbc2_b[par][:, (e % 4) * TC:(e % 4 + 1) * TC].rearrange(
                        "p (o t) -> p o t", o=1).broadcast_to([128, DT, TC])
                    ve.wait_ge(cV.sem, base_V)
                    ve.wait_ge(cS.sem, gbc_S[par])
                    if ue_freeT[wi_]:
                        ve.wait_ge(cT.sem, ue_freeT[wi_])
                    cV.inc(ve.scalar_tensor_tensor(
                        out=ue_b[wi_][:].rearrange("p (k t) -> p k t", t=TC),
                        in0=u2_b[:].rearrange("p (k t) -> p k t", t=TC),
                        scalar=ones1[:], in1=gv, op0=ALU.mult, op1=ALU.mult))
                    ue_V[wi_] = cV.n
                    if e % 4 == 3:
                        gbc_cons[par] = cV.n
                        if e + 8 < NEXP:
                            gbc_chain(tch, e // 4 + 2)
                    mov = ue_b[wi_]
                    te.wait_ge(cV.sem, ue_V[wi_])
                else:
                    mov = uh_b
                    te.wait_ge(cV.sem, base_V)
                te.wait_ge(cL.sem, wr_loaded[wi_])
                last = (ei == len(experts) - 1)
                for dt_i in range(DT):
                    for kt in range(DT):
                        cT.inc(te.matmul(
                            pb[banks_d[dt_i]][:, 0:TC],
                            wr_b[wi_][:, kt * D + dt_i * 128: kt * D + (dt_i + 1) * 128],
                            mov[:, kt * TC:(kt + 1) * TC],
                            start=False, stop=(last and kt == DT - 1)),
                           1 if (dt_i == DT - 1 and kt == DT - 1) else 0)
                ue_freeT[wi_] = cT.n
                wr_freeT[wi_] = cT.n
                if ei + 2 < len(experts):
                    load_wr(experts[ei + 2], (ei + 2) % 2)
            moe_T = cT.n
            ve.wait_ge(cT.sem, moe_T)
            for dt_i in range(DT):
                cV.inc(ve.scalar_tensor_tensor(
                    out=ln2_b[:, dt_i * TC:(dt_i + 1) * TC],
                    in0=pb[banks_d[dt_i]][:, 0:TC], scalar=bsum_c[:, dt_i:dt_i + 1],
                    in1=u2_b[:, dt_i * TC:(dt_i + 1) * TC], op0=ALU.add, op1=ALU.add))
                release_bank(banks_d[dt_i], cV)
            cV.inc(ve.tensor_add(ln2_b[:], ln2_b[:], u2_b[:]))
            sy.wait_ge(cV.sem, cV.n)
            cS.inc(sy.dma_start(out=ffnin_sp[tch], in_=ln2_b[:]), 16)
            emit_ln(ln2_b[:], out_b[:], TC, out_b[:], stat, mu_bc, rs_bc)
            out_V = cV.n
            sy.wait_ge(cV.sem, out_V)
            cS.inc(sy.dma_start(
                out=oT.rearrange("(k p) t -> p k t", p=128)[:, :, tch * TC:(tch + 1) * TC],
                in_=out_b[:].rearrange("p (k t) -> p k t", t=TC)), 16)
            sy.wait_ge(cV.sem, cV.n)

        sy.wait_ge(cS.sem, cS.n)
        ph_barrier()
        esD.close()

    blk.sync(emit)
    es.close()
    return nc


def kernel(L0_attn_x, L0_attn_W_in, L0_attn_conv_w, L0_attn_conv_b,
           L0_attn_W_xproj, L0_attn_W_dt, L0_attn_b_dt, L0_attn_A_log,
           L0_attn_D_param, L0_attn_W_out, L0_ffn_centroids, L0_ffn_expert_biases,
           L0_ffn_Wr, L0_ffn_br, L0_ffn_Ws, L0_ffn_bs, _debug=False, _phases=5):
    x = np.ascontiguousarray(np.asarray(L0_attn_x))
    B = x.shape[0]
    conv_w = np.asarray(L0_attn_conv_w).reshape(C2, 4)
    convd = np.zeros((96, 128, 128), np.float32)
    idx = np.arange(128)
    for tap in range(4):
        for ct in range(CT):
            convd[tap * 16 + ct, idx, idx] = conv_w[ct * 128:(ct + 1) * 128, tap]
    cb = np.asarray(L0_attn_conv_b).reshape(CT, 128)
    bd = np.asarray(L0_attn_b_dt).reshape(CT, 128)
    for ct in range(CT):
        convd[64 + ct, idx, idx] = cb[ct]
        convd[80 + ct, idx, idx] = bd[ct]
    WrT = np.ascontiguousarray(np.transpose(np.asarray(L0_ffn_Wr), (0, 2, 1)))
    WsT = np.ascontiguousarray(np.transpose(np.asarray(L0_ffn_Ws), (0, 2, 1)))
    centT = np.ascontiguousarray(np.asarray(L0_ffn_centroids).T)
    id16f = np.eye(16, dtype=np.float32)

    common = dict(
        W_in=np.ascontiguousarray(L0_attn_W_in), convd=convd,
        conv_b=np.ascontiguousarray(L0_attn_conv_b),
        W_xproj=np.ascontiguousarray(L0_attn_W_xproj),
        W_dt=np.ascontiguousarray(L0_attn_W_dt),
        b_dt=np.ascontiguousarray(L0_attn_b_dt),
        A_log=np.ascontiguousarray(L0_attn_A_log),
        D_param=np.ascontiguousarray(L0_attn_D_param),
        W_out=np.ascontiguousarray(L0_attn_W_out),
        centT=centT, eb=np.ascontiguousarray(L0_ffn_expert_biases),
        WrT=WrT, br=np.ascontiguousarray(L0_ffn_br), WsT=WsT,
        bs=np.ascontiguousarray(L0_ffn_bs), id16f=id16f, id128f=np.eye(128, dtype=np.float32))

    in_maps = []
    for c in range(8):
        b, j = c // 2, c % 2
        if j == 0:
            xv = np.concatenate([np.zeros((TH, D), np.float32), x[b, 0:TH]], 0)
            fl = np.zeros((1, 1), np.float32)
        else:
            xv = np.concatenate([x[b, 0:TH], x[b, TH:]], 0)
            fl = np.ones((1, 1), np.float32)
        m = dict(common)
        m["xT"] = np.ascontiguousarray(xv.T)
        m["flag"] = fl
        in_maps.append(m)

    nc = build_nc(debug=_debug, phases=_phases)
    res = run_bass_kernel_spmd(nc, in_maps, core_ids=list(range(8)))
    out = np.empty((B, TV, D), np.float32)
    for c in range(8):
        b, j = c // 2, c % 2
        out[b, j * TH:(j + 1) * TH, :] = res.results[c]["oT"].T
    if _debug:
        return out, res
    return out
